# version 16
# baseline (speedup 1.0000x reference)
import math
import contextlib
import numpy as np

import concourse.bass as bass
import concourse.mybir as mybir
from concourse.bass_utils import run_bass_kernel_spmd

F32 = mybir.dt.float32
AF = mybir.ActivationFunctionType
ALU = mybir.AluOpType
AX = mybir.AxisListType

NEG_BIG = -1e30
FMAX_NEG = float(np.finfo(np.float32).min)


def full_cfg():
    return dict(S=8192, D=4096, H=32,
                VIS_START=1, VIS_END=513, INSTR_START=513, EXP_ACTIONS=56,
                KEEP_RATIO=0.5, ROPE_THETA=10000.0, N_LORA=8, RANK=16,
                LORA_ALPHA=32)


def mini_cfg():
    return dict(S=2048, D=512, H=4,
                VIS_START=1, VIS_END=257, INSTR_START=257, EXP_ACTIONS=56,
                KEEP_RATIO=0.5, ROPE_THETA=10000.0, N_LORA=8, RANK=16,
                LORA_ALPHA=32)


class _C:
    def __init__(self, cfg):
        self.__dict__.update(cfg)
        self.NCORE = 8
        self.Dh = 128
        self.HID = self.H * self.Dh
        assert self.HID == self.D
        self.TPC = self.S // self.NCORE
        self.NSLOT = self.TPC // 128
        self.CT = self.D // 128
        self.NCH = (self.TPC + 511) // 512
        self.CHW = min(512, self.TPC)
        self.NJC = self.S // 512
        self.NFC = self.D // 512
        self.instr_end = max(self.INSTR_START, self.S - self.EXP_ACTIONS)
        self.V0 = self.VIS_END - self.VIS_START
        assert self.V0 % 128 == 0 and self.V0 <= 512
        self.NVC = self.V0 // 128
        self.keep_now = max(1, min(self.V0, int(round(self.V0 * self.KEEP_RATIO))))
        assert self.N_LORA * self.RANK == 128
        self.LSCALE = float(self.LORA_ALPHA) / self.RANK
        self.KVN = self.HID * self.TPC + self.TPC * self.D
        self.VOFF = self.HID * self.TPC
        self.chunks_per_slot = [min(self.NJC, (128 * (8 * s + 8) + 511) // 512)
                                for s in range(self.NSLOT)]


class Sem:
    def __init__(self, st, nc, name):
        self.h = st.enter_context(nc.semaphore(name))
        self.v = 0


def ap2(t, off, d0, d1):
    return bass.AP(t, off, [list(d0), list(d1)])


def ap3(t, off, d0, d1, d2):
    return bass.AP(t, off, [list(d0), list(d1), list(d2)])


class Prog:
    def __init__(self, cfg):
        self.c = _C(cfg)
        self.nc = bass.Bass("TRN2", target_bir_lowering=False)
        self.st = contextlib.ExitStack()
        self.uid = 0

    def sem(self, name):
        return Sem(self.st, self.nc, name)

    def _n(self, base):
        self.uid += 1
        return f"{base}_{self.uid}"

    def build(self):
        c, nc, st = self.c, self.nc, self.st
        self.x_in = nc.dram_tensor("x", [c.TPC, c.D], F32, kind="ExternalInput")
        self.wq = nc.dram_tensor("wqT", [c.D, c.D], F32, kind="ExternalInput")
        self.wk = nc.dram_tensor("wkT", [c.D, c.D], F32, kind="ExternalInput")
        self.wv = nc.dram_tensor("wvT", [c.D, c.D], F32, kind="ExternalInput")
        self.wo = nc.dram_tensor("woT", [c.D, c.D], F32, kind="ExternalInput")
        self.wf = nc.dram_tensor("WT", [c.D, c.D], F32, kind="ExternalInput")
        self.acat = nc.dram_tensor("AcatT", [c.D, 128], F32, kind="ExternalInput")
        self.bcat = nc.dram_tensor("BcatT", [128, c.D], F32, kind="ExternalInput")
        self.bias_d = nc.dram_tensor("bias", [c.D], F32, kind="ExternalInput")
        self.lw_d = nc.dram_tensor("lw_rep", [128], F32, kind="ExternalInput")
        self.tstart_d = nc.dram_tensor("tstart", [c.NSLOT], F32, kind="ExternalInput")
        self.tmask_d = nc.dram_tensor("textmask", [c.TPC], F32, kind="ExternalInput")
        self.y_out = nc.dram_tensor("y", [c.TPC, c.D], F32, kind="ExternalOutput")
        self.x1_dr = nc.dram_tensor("x1_dr", [c.TPC, c.D], F32)
        self.x2_dr = nc.dram_tensor("x2_dr", [c.TPC, c.D], F32)
        self.q_dr = nc.dram_tensor("q_dr", [c.HID, c.TPC], F32)
        self.at_dr = nc.dram_tensor("at_dr", [c.HID, c.TPC], F32)
        self.kv_in = nc.dram_tensor("kv_in", [c.KVN], F32)
        self.kv_out = nc.dram_tensor("kv_out", [c.NCORE * c.KVN], F32)
        self.qb_in = nc.dram_tensor("qb_in", [128 * c.H], F32)
        self.qb_out = nc.dram_tensor("qb_out", [c.NCORE * 128 * c.H], F32)

        sb = lambda n, s: st.enter_context(nc.sbuf_tensor(n, s, F32))
        self.iden = sb("iden", [128, 128])
        self.ones = sb("ones", [1, 128])
        self.piota = sb("piota", [128, 1])
        self.bias_sb = sb("bias_sb", [128, c.CT])
        self.hbias = sb("hbias", [128, c.CT])
        self.coefs = sb("coefs", [128, 1])
        self.qbar = sb("qbar", [128, c.H])
        self.stats = sb("stats", [128, 24])
        self.sc1 = sb("sc1", [1, 512])
        self.wslots = sb("wslots", [128, 8 * 128])
        self.NWS = 8

        ps = lambda n, s: st.enter_context(nc.psum_tensor(n, s, F32))
        self.psA = ps("psA", [128, 512])
        self.psA2 = ps("psA2", [128, 512])
        self.psB = ps("psB", [128, 512])
        self.psB2 = ps("psB2", [128, 512])
        self.psT1 = ps("psT1", [128, 512])
        self.psT2 = ps("psT2", [128, 512])
        self.psAt = ps("psAt", [128, 512])
        self.psM = ps("psM", [128, 512])

        self.s_ld = self.sem("s_ld")
        self.s_x = self.sem("s_x")
        self.s_kv = self.sem("s_kv")
        self.s_zl = self.sem("s_zl")
        self.s_kvi = self.sem("s_kvi")
        self.s_ep = self.sem("s_ep")
        self.s_wl = [self.sem(f"s_wl{i}") for i in range(8)]
        self.wl_cnt = [0] * 8
        self.unit_rel = []
        self.s_st = self.sem("s_st")
        self.s_gd = self.sem("s_gd")
        self.s_t = self.sem("s_t")
        self.s_v = self.sem("s_v")
        self.s_a = self.sem("s_a")
        self.s_g = self.sem("s_g")
        self.s_wf = self.sem("s_wf")
        self.s_cc = self.sem("s_cc")
        self.s_bar = self.sem("s_bar")
        self.wload = 0

        with self.st:
            self.p0_setup()
            self.barrier()
            with contextlib.ExitStack() as g:
                self.alloc_proj(g, lora=True)
                self.ffn_phase(self.x_in, self.x_in, self.x1_dr)
                self.barrier()
            with contextlib.ExitStack() as g:
                self.alloc_proj(g, lora=False, qkv=True)
                self.p2_qkv()
                self.barrier()
            with contextlib.ExitStack() as g:
                self.alloc_span(g)
                with contextlib.ExitStack() as g3:
                    self.alloc_p3(g3)
                    self.p3_topk()
                    self.barrier()
                with contextlib.ExitStack() as g4:
                    self.alloc_p4(g4)
                    self.p4_attn()
                    self.barrier()
            with contextlib.ExitStack() as g:
                self.alloc_proj(g, lora=False)
                self.p5_oproj()
                self.barrier()
            with contextlib.ExitStack() as g:
                self.alloc_proj(g, lora=True)
                self.ffn_phase(self.x2_dr, self.x2_dr, self.y_out)
            self.p7_finish()
        return self.nc

    def alloc_proj(self, g, lora, qkv=False):
        c, nc = self.c, self.nc
        sb = lambda n, s: g.enter_context(nc.sbuf_tensor(self._n(n), s, F32))
        sbb = lambda n, s: g.enter_context(nc.sbuf_tensor(self._n(n), s, mybir.dt.bfloat16))
        self.z_hi = sbb("z_hi", [128, c.CT * c.TPC])
        self.z_lo = sbb("z_lo", [128, c.CT * c.TPC])
        self.w_hi = sbb("w_hi", [128, 8 * 128])
        self.w_lo = sbb("w_lo", [128, 8 * 128])
        self.xt_buf = sb("xt_buf", [128, c.D])
        self.zt_buf = sb("zt_buf", [128, c.D])
        self.ffn_buf = sb("ffn_buf", [128, 1024 if qkv else 2 * max(c.TPC, 512)])
        if not qkv:
            self.ep_x = sb("ep_x", [128, c.NSLOT * 128])
            self.ep_o = sb("ep_o", [128, c.NSLOT * 128])
        self.bn_buf = sb("bn_buf", [128, 6 * ((c.D + 511) // 512)])
        if lora:
            self.hsc_hi = sbb("hsc_hi", [128, c.TPC])
            self.hsc_lo = sbb("hsc_lo", [128, c.TPC])
        if qkv:
            self.cos_t = sb("cos_t", [128, c.TPC])
            self.sin_t = sb("sin_t", [128, c.TPC])
            self.tmask_b = sb("tmask_b", [128, c.TPC])
            self.qrow = sb("qrow", [128, c.TPC])
            self.krow = sb("krow", [128, c.TPC])
            self.rope_a = sb("rope_a", [128, c.TPC])
            self.rope_b = sb("rope_b", [128, c.TPC])

    def alloc_span(self, g):
        c, nc = self.c, self.nc
        sb = lambda n, s: g.enter_context(nc.sbuf_tensor(self._n(n), s, F32))
        self.maskt = sb("maskt", [128, (2 + 2 * c.NSLOT) * 512])
        self.iabs = sb("iabs", [128, c.NSLOT])
        self.tst_b = sb("tst_b", [128, c.NSLOT])

    def alloc_p3(self, g):
        c, nc = self.c, self.nc
        sb = lambda n, s: g.enter_context(nc.sbuf_tensor(self._n(n), s, F32))
        self.kvis = sb("kvis", [128, (c.NVC + 1) * 128])
        self.colv = sb("colv", [1, c.S])
        self.qb_all = sb("qb_all", [128, c.NCORE * c.H])
        self.impb = sb("impb", [128, c.V0])
        self.impc = sb("impc", [128, c.NVC])
        self.rankb = sb("rankb", [128, c.NVC])
        self.tieb = sb("tieb", [128, c.NVC])
        self.keepc = sb("keepc", [128, c.NVC])
        self.uio = sb("uio", [128, c.V0])
        self.vcol = sb("vcol", [128, c.NVC])
        self.g_scr = sb("g_scr", [128, c.V0])
        self.g2_scr = sb("g2_scr", [128, c.V0])
        self.keeprow = sb("keeprow", [c.NVC, 128])
        self.sc3 = sb("sc3", [1, 512])

    def alloc_p4(self, g):
        c, nc = self.c, self.nc
        sb = lambda n, s: g.enter_context(nc.sbuf_tensor(self._n(n), s, F32))
        self.kh = sb("kh", [128, c.S])
        self.vh = sb("vh", [128, c.S])
        self.qh = sb("qh", [128, c.TPC])
        self.scores = sb("scores", [128, 2 * c.S])
        self.aout = sb("aout", [128, c.TPC])
        self.ptbuf = sb("ptbuf", [128, 2 * 128])
        self.cmax = sb("cmax", [128, 2 * c.NJC])
        self.nmax = sb("nmax", [128, 2])
        self.den = sb("den", [128, 8])

    def barrier(self):
        bar = self.s_bar
        target = bar.v + 5
        with self.nc.Block() as blk:
            @blk.sync
            def _(sy):
                for s in (self.s_st, self.s_gd, self.s_ld, self.s_x, self.s_ep):
                    if s.v > 0:
                        sy.wait_ge(s.h, s.v)
                sy.sem_inc(bar.h, 1)
                sy.wait_ge(bar.h, target)
            for name in ("tensor", "vector", "scalar", "gpsimd"):
                def body(e, _t=target):
                    e.sem_inc(bar.h, 1)
                    e.wait_ge(bar.h, _t)
                getattr(blk, name)(body)
        bar.v = target

    def p0_setup(self):
        c, nc = self.c, self.nc
        s_ld, s_g, s_v = self.s_ld, self.s_g, self.s_v
        with nc.Block() as blk:
            ld0 = s_ld.v

            @blk.sync
            def _(sy):
                sy.dma_start(out=self.bias_sb[:, :],
                             in_=ap2(self.bias_d, 0, (1, 128), (128, c.CT)),
                             allow_slow_non_contiguous=True).then_inc(s_ld.h, 16)
                sy.dma_start(out=self.coefs[:, :],
                             in_=ap2(self.lw_d, 0, (1, 128), (1, 1))).then_inc(s_ld.h, 16)
            s_ld.v += 32

            @blk.gpsimd
            def _(g):
                g.memset(self.iden[:, :], 0.0)
                g.memset(self.ones[:, :], 1.0)
                g.iota(self.piota[:, :], [[0, 1]], base=0, channel_multiplier=1,
                       allow_small_or_imprecise_dtypes=True).then_inc(s_g.h, 1)
                g.wait_ge(s_g.h, s_g.v + 1)
                g.affine_select(out=self.iden[:, :], in_=self.iden[:, :],
                                compare_op=ALU.not_equal, fill=1.0, base=0,
                                pattern=[[-1, 128]], channel_multiplier=1
                                ).then_inc(s_g.h, 1)
            s_g.v += 2

            @blk.vector
            def _(v, ld0=ld0):
                v.wait_ge(s_ld.h, ld0 + 32)
                v.tensor_scalar(out=self.hbias[:, :], in0=self.bias_sb[:, :],
                                scalar1=0.5, scalar2=None, op0=ALU.mult)
                v.tensor_scalar(out=self.coefs[:, :], in0=self.coefs[:, :],
                                scalar1=float(c.LSCALE), scalar2=None, op0=ALU.mult
                                ).then_inc(s_v.h, 1)
            s_v.v += 1

    def z_build(self, blk, src_dram):
        c, nc = self.c, self.nc
        s_ld, s_t, s_v, s_a = self.s_ld, self.s_t, self.s_v, self.s_a
        nbn = (c.D + 511) // 512
        for k in range(c.NSLOT):
            if True:
                ld0 = self.s_x.v
                a_pre = s_a.v

                @blk.sync
                def _(sy, k=k, a_pre=a_pre):
                    if k > 0:
                        sy.wait_ge(self.s_a.h, a_pre)
                    sy.dma_start(out=self.xt_buf[:, :],
                                 in_=self.x_like(src_dram, k)).then_inc(self.s_x.h, 16)
                self.s_x.v += 16

                v_pre = s_v.v

                @blk.vector
                def _(v, ld0=ld0, v_pre=v_pre):
                    v.wait_ge(self.s_x.h, ld0 + 16)
                    cnt = v_pre

                    def step(op):
                        nonlocal cnt
                        op().then_inc(s_v.h, 1)
                        cnt += 1
                        v.wait_ge(s_v.h, cnt)
                    for i in range(nbn):
                        w = min(512, c.D - 512 * i)
                        inst = v.bn_stats(self.bn_buf[:, 6 * i:6 * i + 6],
                                          self.xt_buf[:, 512 * i:512 * i + w])
                    inst.then_inc(s_v.h, 1)
                    cnt += 1
                    v.wait_ge(s_v.h, cnt)
                    step(lambda: v.bn_aggr(self.stats[:, 0:2], self.bn_buf[:, 0:6 * nbn]))
                    step(lambda: v.tensor_scalar(out=self.stats[:, 2:3], in0=self.stats[:, 1:2],
                                                 scalar1=1e-5, scalar2=None, op0=ALU.add))
                s_v.v = v_pre + 3

                @blk.scalar
                def _(s):
                    s.wait_ge(s_v.h, s_v.v)
                    s.activation(self.stats[:, 3:4], self.stats[:, 2:3], AF.Sqrt
                                 ).then_inc(s_a.h, 1)
                s_a.v += 1

                v_pre = s_v.v

                @blk.vector
                def _(v, v_pre=v_pre):
                    v.wait_ge(s_a.h, s_a.v)
                    cnt = v_pre

                    def step(op):
                        nonlocal cnt
                        op().then_inc(s_v.h, 1)
                        cnt += 1
                        v.wait_ge(s_v.h, cnt)
                    step(lambda: v.reciprocal(self.stats[:, 4:5], self.stats[:, 3:4]))
                    yc = 4
                    for it in range(2):
                        step(lambda yc=yc: v.tensor_tensor(
                            out=self.stats[:, 5:6], in0=self.stats[:, yc:yc + 1],
                            in1=self.stats[:, yc:yc + 1], op=ALU.mult))
                        step(lambda: v.tensor_tensor(
                            out=self.stats[:, 6:7], in0=self.stats[:, 5:6],
                            in1=self.stats[:, 2:3], op=ALU.mult))
                        step(lambda: v.tensor_scalar(
                            out=self.stats[:, 6:7], in0=self.stats[:, 6:7],
                            scalar1=-0.5, scalar2=1.5, op0=ALU.mult, op1=ALU.add))
                        step(lambda yc=yc, it=it: v.tensor_tensor(
                            out=self.stats[:, 7 + it:8 + it], in0=self.stats[:, yc:yc + 1],
                            in1=self.stats[:, 6:7], op=ALU.mult))
                        yc = 7 + it
                    step(lambda: v.tensor_tensor(out=self.stats[:, 9:10],
                                                 in0=self.stats[:, 0:1],
                                                 in1=self.stats[:, 8:9], op=ALU.mult))
                    step(lambda: v.tensor_scalar(out=self.stats[:, 10:11],
                                                 in0=self.stats[:, 9:10],
                                                 scalar1=-1.0, scalar2=None, op0=ALU.mult))
                s_v.v = v_pre + 11

                @blk.scalar
                def _(s):
                    s.wait_ge(s_v.h, s_v.v)
                    s.activation(self.zt_buf[:, :], self.xt_buf[:, :], AF.Identity,
                                 bias=self.stats[:, 10:11], scale=self.stats[:, 8:9]
                                 ).then_inc(s_a.h, 1)
                s_a.v += 1

                t_pre, v_pre = s_t.v, s_v.v

                @blk.tensor
                def _(t, k=k, t_pre=t_pre, v_pre=v_pre):
                    t.wait_ge(s_a.h, s_a.v)
                    for cj in range(c.CT):
                        pst = self.psT1 if cj % 2 == 0 else self.psT2
                        t.wait_ge(s_v.h, max(0, v_pre + 2 * (cj - 2) + 2 if cj >= 2 else v_pre - 1))
                        t.transpose(pst[:, 0:128], self.zt_buf[:, 128 * cj:128 * cj + 128],
                                    self.iden[:, :]).then_inc(s_t.h, 1)

                @blk.vector
                def _(v, k=k, t_pre=t_pre, v_pre=v_pre):
                    cnt2 = v_pre
                    for cj in range(c.CT):
                        pst = self.psT1 if cj % 2 == 0 else self.psT2
                        sl = slice(cj * c.TPC + 128 * k, cj * c.TPC + 128 * k + 128)
                        v.wait_ge(s_t.h, t_pre + cj + 1)
                        v.tensor_copy(self.z_hi[:, sl], pst[:, 0:128]).then_inc(s_v.h, 1)
                        cnt2 += 1
                        v.wait_ge(s_v.h, cnt2)
                        v.scalar_tensor_tensor(out=self.z_lo[:, sl], in0=self.z_hi[:, sl],
                                               scalar=-1.0, in1=pst[:, 0:128],
                                               op0=ALU.mult, op1=ALU.add).then_inc(s_v.h, 1)
                        cnt2 += 1
                s_t.v += c.CT
                s_v.v += 2 * c.CT

    def x_like(self, src, k, fj=None):
        c = self.c
        if fj is None:
            return ap2(src, 128 * k * c.D, (c.D, 128), (1, c.D))
        return ap2(src, 128 * k * c.D + 128 * fj, (c.D, 128), (1, 128))

    def w_load(self, sy, dram, cj, fj, rowstride=None):
        c = self.c
        rs = c.D if rowstride is None else rowstride
        i = self.wload
        b = i - self.NWS
        if b >= 0 and self.unit_rel[b] > 0:
            sy.wait_ge(self.s_wf.h, self.unit_rel[b])
        slot = i % self.NWS
        sy.dma_start(out=self.wslots[:, 128 * slot:128 * slot + 128],
                     in_=ap2(dram, 128 * cj * rs + 128 * fj, (rs, 128), (1, 128))
                     ).then_inc(self.s_wl[slot].h, 16)
        self.wl_cnt[slot] += 16
        self.unit_rel.append(i + 1)
        self.wload += 1
        return slot, self.wl_cnt[slot]

    def ffn_phase(self, src_x, res_x, dst):
        c, nc = self.c, self.nc
        s_ld, s_st, s_t, s_v = self.s_ld, self.s_st, self.s_t, self.s_v
        with nc.Block() as blk:
            self.z_build(blk, src_x)
            loads = []

            @blk.sync
            def _(sy):
                for cj in range(c.CT):
                    loads.append(self.w_load(sy, self.acat, cj, 0, rowstride=128))

            v_pre = s_v.v
            splits = [v_pre + 2 * (i + 1) for i in range(c.CT)]

            @blk.vector
            def _(v, v_pre=v_pre):
                cnt = v_pre
                for i in range(c.CT):
                    slot, lv = loads[i]
                    v.wait_ge(self.s_wl[slot].h, lv)
                    ws = self.wslots[:, 128 * slot:128 * slot + 128]
                    v.tensor_copy(self.w_hi[:, 128 * slot:128 * slot + 128], ws
                                  ).then_inc(s_v.h, 1)
                    cnt += 1
                    v.wait_ge(s_v.h, cnt)
                    v.scalar_tensor_tensor(out=self.w_lo[:, 128 * slot:128 * slot + 128],
                                           in0=self.w_hi[:, 128 * slot:128 * slot + 128],
                                           scalar=-1.0, in1=ws,
                                           op0=ALU.mult, op1=ALU.add).then_inc(s_v.h, 1)
                    cnt += 1
            s_v.v += 2 * c.CT
            z_gate0 = s_v.v

            @blk.tensor
            def _(t):
                for cj in range(c.CT):
                    slot, lv = loads[cj]
                    t.wait_ge(s_v.h, splits[cj])
                    whi = self.w_hi[:, 128 * slot:128 * slot + 128]
                    wlo = self.w_lo[:, 128 * slot:128 * slot + 128]
                    sls = [slice(cj * c.TPC + ch * 512, cj * c.TPC + ch * 512 + c.CHW)
                           for ch in range(c.NCH)]
                    pbs = [self.psB if ch == 0 else self.psB2 for ch in range(c.NCH)]
                    for ch in range(c.NCH):
                        t.matmul(pbs[ch][:, 0:c.CHW], whi, self.z_hi[:, sls[ch]],
                                 start=(cj == 0), stop=False)
                        t.matmul(pbs[ch][:, 0:c.CHW], whi, self.z_lo[:, sls[ch]],
                                 start=False, stop=False)
                    for ch in range(c.NCH):
                        inst = t.matmul(pbs[ch][:, 0:c.CHW], wlo, self.z_hi[:, sls[ch]],
                                        start=False, stop=(cj == c.CT - 1))
                    inst.then_inc(self.s_wf.h, 1)
            wf_done = self.wload

            @blk.vector
            def _(v, wf_done=wf_done):
                v.wait_ge(self.s_wf.h, wf_done)
                cnt = s_v.v
                for ch in range(c.NCH):
                    psb = self.psB if ch == 0 else self.psB2
                    tb = self.ffn_buf[:, ch * 512:ch * 512 + c.CHW]
                    hh = self.hsc_hi[:, ch * 512:ch * 512 + c.CHW]
                    hl = self.hsc_lo[:, ch * 512:ch * 512 + c.CHW]
                    v.tensor_scalar(out=tb, in0=psb[:, 0:c.CHW],
                                    scalar1=self.coefs[:, 0:1], scalar2=None,
                                    op0=ALU.mult).then_inc(s_v.h, 1)
                    cnt += 1
                    v.wait_ge(s_v.h, cnt)
                    v.tensor_copy(hh, tb).then_inc(s_v.h, 1)
                    cnt += 1
                    v.wait_ge(s_v.h, cnt)
                    v.scalar_tensor_tensor(out=hl, in0=hh, scalar=-1.0, in1=tb,
                                           op0=ALU.mult, op1=ALU.add).then_inc(s_v.h, 1)
                    cnt += 1
            s_v.v += 3 * c.NCH
            self.proj_pass(blk, self.wf, dst, res_x, half_bias=True, lora=True)

    def proj_mm(self, blk, wdram, fj, half_bias, lora, pst8):
        c = self.c
        s_ld, s_t, s_v = self.s_ld, self.s_t, self.s_v
        psA = self.psA if fj % 2 == 0 else self.psB
        psA2 = self.psA2 if fj % 2 == 0 else self.psB2
        loads = []

        @blk.sync
        def _(sy, fj=fj):
            for cj in range(c.CT):
                loads.append(self.w_load(sy, wdram, cj, fj))
            if lora:
                loads.append(self.w_load(sy, self.bcat, 0, fj, rowstride=c.D))

        nld = len(loads) if loads else c.CT + (1 if lora else 0)
        v_pre = s_v.v
        splits = [v_pre + 2 * (i + 1) for i in range(c.CT + (1 if lora else 0))]

        @blk.vector
        def _(v, fj=fj, v_pre=v_pre):
            cnt = v_pre
            for i in range(c.CT + (1 if lora else 0)):
                slot, lv = loads[i]
                v.wait_ge(self.s_wl[slot].h, lv)
                ws = self.wslots[:, 128 * slot:128 * slot + 128]
                v.tensor_copy(self.w_hi[:, 128 * slot:128 * slot + 128], ws
                              ).then_inc(s_v.h, 1)
                cnt += 1
                v.wait_ge(s_v.h, cnt)
                v.scalar_tensor_tensor(out=self.w_lo[:, 128 * slot:128 * slot + 128],
                                       in0=self.w_hi[:, 128 * slot:128 * slot + 128],
                                       scalar=-1.0, in1=ws,
                                       op0=ALU.mult, op1=ALU.add).then_inc(s_v.h, 1)
                cnt += 1
        s_v.v += 2 * (c.CT + (1 if lora else 0))

        mm_gate = max(pst8["z_gate"],
                      pst8["evac_inc"][fj - 2] if fj >= 2 else 0)

        @blk.tensor
        def _(t, fj=fj, mm_gate=mm_gate):
            t.wait_ge(s_v.h, mm_gate)
            first = True
            for cj in range(c.CT):
                slot, lv = loads[cj]
                t.wait_ge(s_v.h, splits[cj])
                whi = self.w_hi[:, 128 * slot:128 * slot + 128]
                wlo = self.w_lo[:, 128 * slot:128 * slot + 128]
                last = (cj == c.CT - 1) and not lora
                sls = [slice(cj * c.TPC + ch * 512, cj * c.TPC + ch * 512 + c.CHW)
                       for ch in range(c.NCH)]
                pas = [psA if ch == 0 else psA2 for ch in range(c.NCH)]
                for ch in range(c.NCH):
                    t.matmul(pas[ch][:, 0:c.CHW], whi, self.z_hi[:, sls[ch]],
                             start=(cj == 0), stop=False)
                    t.matmul(pas[ch][:, 0:c.CHW], whi, self.z_lo[:, sls[ch]],
                             start=False, stop=False)
                for ch in range(c.NCH):
                    inst = t.matmul(pas[ch][:, 0:c.CHW], wlo, self.z_hi[:, sls[ch]],
                                    start=False, stop=last)
                inst.then_inc(self.s_wf.h, 1)
            if lora:
                slot, lv = loads[c.CT]
                t.wait_ge(s_v.h, splits[c.CT])
                whi = self.w_hi[:, 128 * slot:128 * slot + 128]
                wlo = self.w_lo[:, 128 * slot:128 * slot + 128]
                sls = [slice(ch * 512, ch * 512 + c.CHW) for ch in range(c.NCH)]
                pas = [psA if ch == 0 else psA2 for ch in range(c.NCH)]
                for ch in range(c.NCH):
                    t.matmul(pas[ch][:, 0:c.CHW], whi, self.hsc_hi[:, sls[ch]],
                             start=False, stop=False)
                    t.matmul(pas[ch][:, 0:c.CHW], whi, self.hsc_lo[:, sls[ch]],
                             start=False, stop=False)
                for ch in range(c.NCH):
                    inst = t.matmul(pas[ch][:, 0:c.CHW], wlo, self.hsc_hi[:, sls[ch]],
                                    start=False, stop=True)
                inst.then_inc(self.s_wf.h, 1)
        wf_done = self.wload
        fb = max(self.c.TPC, 512) * (fj % 2)

        @blk.vector
        def _(v, fj=fj, wf_done=wf_done, fb=fb):
            v.wait_ge(self.s_wf.h, wf_done)
            if fj >= 2:
                v.wait_ge(s_t.h, pst8["transp_end"][fj - 2])
            for ch in range(c.NCH):
                pa = psA if ch == 0 else psA2
                if half_bias:
                    inst = v.tensor_scalar(out=self.ffn_buf[:, fb + ch * 512:fb + ch * 512 + c.CHW],
                                           in0=pa[:, 0:c.CHW], scalar1=0.5,
                                           scalar2=self.hbias[:, fj:fj + 1],
                                           op0=ALU.mult, op1=ALU.add)
                else:
                    inst = v.tensor_copy(self.ffn_buf[:, fb + ch * 512:fb + ch * 512 + c.CHW],
                                         pa[:, 0:c.CHW])
            inst.then_inc(s_v.h, 1)
        s_v.v += 1
        pst8["evac_inc"].append(s_v.v)

    def proj_ep(self, blk, fj, dst, res_x, pst8):
        c = self.c
        s_ld, s_st, s_t, s_v = self.s_ld, self.s_st, self.s_t, self.s_v
        fb = max(self.c.TPC, 512) * (fj % 2)
        ep_gate = pst8["add_end"][fj - 1] if fj >= 1 else 0

        @blk.sync
        def _(sy, fj=fj, ep_gate=ep_gate):
            if ep_gate > 0:
                sy.wait_ge(s_v.h, ep_gate)
            for k in range(c.NSLOT):
                sy.dma_start(out=self.ep_x[:, 128 * k:128 * k + 128],
                             in_=self.x_like(res_x, k, fj)).then_inc(self.s_ep.h, 16)
                self.s_ep.v += 16
        epl_end = self.s_ep.v
        t_pre, v_pre = s_t.v, s_v.v
        ev = pst8["evac_inc"][fj]

        @blk.tensor
        def _(t, fj=fj, t_pre=t_pre, v_pre=v_pre, ev=ev):
            t.wait_ge(s_v.h, ev)
            for k in range(c.NSLOT):
                pst = self.psT1 if k % 2 == 0 else self.psT2
                t.wait_ge(s_v.h, max(0, v_pre + k - 1))
                t.transpose(pst[:, 0:128], self.ffn_buf[:, fb + 128 * k:fb + 128 * k + 128],
                            self.iden[:, :]).then_inc(s_t.h, 1)
        s_t.v += c.NSLOT
        pst8["transp_end"].append(s_t.v)
        st_pre = s_st.v

        @blk.vector
        def _(v, fj=fj, t_pre=t_pre, st_pre=st_pre, epl_end=epl_end):
            v.wait_ge(self.s_ep.h, epl_end)
            v.wait_ge(s_st.h, st_pre)
            for k in range(c.NSLOT):
                pst = self.psT1 if k % 2 == 0 else self.psT2
                v.wait_ge(s_t.h, t_pre + k + 1)
                v.tensor_add(self.ep_o[:, 128 * k:128 * k + 128],
                             pst[:, 0:128],
                             self.ep_x[:, 128 * k:128 * k + 128]
                             ).then_inc(s_v.h, 1)
        base_v = s_v.v
        s_v.v += c.NSLOT
        pst8["add_end"].append(s_v.v)

        @blk.scalar
        def _(s, fj=fj, base_v=base_v):
            for k in range(c.NSLOT):
                s.wait_ge(s_v.h, base_v + k + 1)
                s.dma_start(out=self.x_like(dst, k, fj),
                            in_=self.ep_o[:, 128 * k:128 * k + 128]
                            ).then_inc(s_st.h, 16)
        s_st.v += 16 * c.NSLOT

    def proj_pass(self, blk, wdram, dst, res_x, half_bias, lora):
        c = self.c
        pst8 = {"z_gate": self.s_v.v, "evac_inc": [], "transp_end": [], "add_end": []}
        for fj in range(c.CT):
            self.proj_mm(blk, wdram, fj, half_bias, lora, pst8)
            if fj >= 1:
                self.proj_ep(blk, fj - 1, dst, res_x, pst8)
        self.proj_ep(blk, c.CT - 1, dst, res_x, pst8)

    def p2_qkv(self):
        c, nc = self.c, self.nc
        s_ld, s_st, s_gd, s_t, s_v, s_a, s_g = (self.s_ld, self.s_st, self.s_gd,
                                                self.s_t, self.s_v, self.s_a, self.s_g)
        with nc.Block() as blk:
            self.z_build(blk, self.x1_dr)
            ld0 = s_ld.v

            @blk.sync
            def _(sy):
                sy.dma_start(out=self.sc1[:, 0:c.NSLOT],
                             in_=ap2(self.tstart_d, 0, (c.NSLOT, 1), (1, c.NSLOT))
                             ).then_inc(s_ld.h, 16)
                sy.dma_start(out=self.qrow[0:1, 0:c.TPC],
                             in_=ap2(self.tmask_d, 0, (c.TPC, 1), (1, c.TPC))
                             ).then_inc(s_ld.h, 16)
            s_ld.v += 32

            g_pre = s_g.v

            @blk.gpsimd
            def _(g, ld0=ld0):
                g.wait_ge(s_ld.h, ld0 + 32)
                for s in range(c.NSLOT):
                    g.iota(self.krow[0:1, 128 * s:128 * s + 128], [[1, 128]], base=0,
                           channel_multiplier=0, allow_small_or_imprecise_dtypes=True)
                g.iota(self.stats[0:64, 16:17], [[0, 1]], base=0, channel_multiplier=1,
                       allow_small_or_imprecise_dtypes=True)
                g.iota(self.stats[64:128, 16:17], [[0, 1]], base=0, channel_multiplier=1,
                       allow_small_or_imprecise_dtypes=True).then_inc(s_g.h, 1)
            s_g.v += 1

            v_pre = s_v.v

            @blk.vector
            def _(v, v_pre=v_pre):
                v.wait_ge(s_g.h, s_g.v)
                cnt = v_pre

                def step(op):
                    nonlocal cnt
                    op().then_inc(s_v.h, 1)
                    cnt += 1
                    v.wait_ge(s_v.h, cnt)
                for s in range(c.NSLOT):
                    inst = v.tensor_scalar(out=self.krow[0:1, 128 * s:128 * s + 128],
                                           in0=self.krow[0:1, 128 * s:128 * s + 128],
                                           scalar1=self.sc1[0:1, s:s + 1], scalar2=None,
                                           op0=ALU.add)
                inst.then_inc(s_v.h, 1)
                cnt += 1
                v.wait_ge(s_v.h, cnt)
                step(lambda: v.tensor_scalar(out=self.stats[:, 17:18],
                                             in0=self.stats[:, 16:17],
                                             scalar1=-math.log(c.ROPE_THETA) / 64.0,
                                             scalar2=None, op0=ALU.mult))
            s_v.v = v_pre + 2

            t_pre = s_t.v

            @blk.tensor
            def _(t, t_pre=t_pre):
                t.wait_ge(s_v.h, s_v.v)
                n = 0
                for src, n0 in ((self.krow, 0), (self.qrow, c.NCH)):
                    for ch in range(c.NCH):
                        pa = self.psT1 if (n0 + ch) % 2 == 0 else self.psT2
                        t.wait_ge(s_v.h, max(0, s_v.v + n0 + ch - 1))
                        t.matmul(pa[:, 0:c.CHW], self.ones[0:1, :],
                                 src[0:1, ch * 512:ch * 512 + c.CHW],
                                 start=True, stop=True).then_inc(s_t.h, 1)

            @blk.vector
            def _(v, t_pre=t_pre):
                for dst, n0 in ((self.rope_a, 0), (self.tmask_b, c.NCH)):
                    for ch in range(c.NCH):
                        pa = self.psT1 if (n0 + ch) % 2 == 0 else self.psT2
                        v.wait_ge(s_t.h, t_pre + n0 + ch + 1)
                        v.tensor_copy(dst[:, ch * 512:ch * 512 + c.CHW], pa[:, 0:c.CHW]
                                      ).then_inc(s_v.h, 1)
            s_t.v += 2 * c.NCH
            s_v.v += 2 * c.NCH

            @blk.scalar
            def _(s):
                s.wait_ge(s_v.h, s_v.v)
                s.activation(self.stats[:, 18:19], self.stats[:, 17:18], AF.Exp
                             ).then_inc(s_a.h, 1)
            s_a.v += 1

            v_pre = s_v.v

            @blk.vector
            def _(v, v_pre=v_pre):
                v.wait_ge(s_a.h, s_a.v)
                cnt = v_pre

                def step(op):
                    nonlocal cnt
                    op().then_inc(s_v.h, 1)
                    cnt += 1
                    v.wait_ge(s_v.h, cnt)
                step(lambda: v.tensor_scalar(out=self.rope_b[:, :], in0=self.rope_a[:, :],
                                             scalar1=self.stats[:, 18:19], scalar2=None,
                                             op0=ALU.mult))
                step(lambda: v.tensor_scalar(out=self.stats[:, 19:20], in0=self.piota[:, :],
                                             scalar1=63.5, scalar2=2.0,
                                             op0=ALU.is_gt, op1=ALU.mult))
                step(lambda: v.tensor_scalar(out=self.stats[:, 19:20],
                                             in0=self.stats[:, 19:20],
                                             scalar1=-1.0, scalar2=None, op0=ALU.add))
            s_v.v = v_pre + 3

            for which in ("sin", "cos"):
                v_pre = s_v.v

                @blk.vector
                def _(v, which=which, v_pre=v_pre):
                    v.wait_ge(s_t.h, s_t.v)
                    v.wait_ge(s_a.h, s_a.v)
                    cnt = v_pre

                    def step(op):
                        nonlocal cnt
                        op().then_inc(s_v.h, 1)
                        cnt += 1
                        v.wait_ge(s_v.h, cnt)
                    add = 0.25 if which == "cos" else 0.0
                    step(lambda: v.tensor_scalar(out=self.rope_a[:, :], in0=self.rope_b[:, :],
                                                 scalar1=1.0 / (2.0 * math.pi), scalar2=add,
                                                 op0=ALU.mult, op1=ALU.add))
                    p = 1 << 11
                    while p >= 1:
                        step(lambda p=p: v.tensor_scalar(
                            out=self.zt_buf[:, 0:c.TPC], in0=self.rope_a[:, :],
                            scalar1=float(p), scalar2=None, op0=ALU.is_ge))
                        step(lambda p=p: v.scalar_tensor_tensor(
                            out=self.rope_a[:, :], in0=self.zt_buf[:, 0:c.TPC],
                            scalar=-float(p), in1=self.rope_a[:, :],
                            op0=ALU.mult, op1=ALU.add))
                        p //= 2
                    step(lambda: v.tensor_scalar(out=self.zt_buf[:, 0:c.TPC],
                                                 in0=self.rope_a[:, :],
                                                 scalar1=0.5, scalar2=None, op0=ALU.is_ge))
                    step(lambda: v.tensor_tensor(out=self.rope_a[:, :], in0=self.rope_a[:, :],
                                                 in1=self.zt_buf[:, 0:c.TPC],
                                                 op=ALU.subtract))
                    step(lambda: v.tensor_scalar(out=self.rope_a[:, :], in0=self.rope_a[:, :],
                                                 scalar1=2.0 * math.pi, scalar2=None,
                                                 op0=ALU.mult))
                s_v.v = v_pre + 28

                @blk.scalar
                def _(s, which=which):
                    s.wait_ge(s_v.h, s_v.v)
                    dst = self.sin_t if which == "sin" else self.cos_t
                    s.activation(dst[:, :], self.rope_a[:, :], AF.Sin
                                 ).then_inc(s_a.h, 1)
                s_a.v += 1

            @blk.vector
            def _(v):
                v.wait_ge(s_a.h, s_a.v)
                v.tensor_scalar(out=self.sin_t[:, :], in0=self.sin_t[:, :],
                                scalar1=self.stats[:, 19:20], scalar2=None, op0=ALU.mult
                                ).then_inc(s_v.h, 1)
            s_v.v += 1

            self.head_evac = []
            self.head_gate0 = s_v.v
            for h in range(c.H):
                self.proj_head(blk, self.wk, h, is_q=False)
            self.v_proj(blk)
            kv_st = s_st.v

            @blk.gpsimd
            def _(g, kv_st=kv_st):
                g.wait_ge(s_st.h, kv_st)
                g.collective_compute(
                    "AllGather", ALU.bypass, replica_groups=[list(range(c.NCORE))],
                    ins=[self.kv_in[:].opt()], outs=[self.kv_out[:].opt()],
                ).then_inc(self.s_cc.h, 1)
            self.s_cc.v += 1
            self.head_evac = []
            self.head_gate0 = s_v.v
            for h in range(c.H):
                self.proj_head(blk, self.wq, h, is_q=True)
            qv_done = s_v.v

            @blk.scalar
            def _(s, qv_done=qv_done):
                s.wait_ge(s_v.h, qv_done)
                s.dma_start(out=ap2(self.qb_in, 0, (c.H, 128), (1, c.H)),
                            in_=self.qbar[:, :]).then_inc(s_st.h, 16)
            s_st.v += 16
            qb_st = s_st.v

            @blk.gpsimd
            def _(g, qb_st=qb_st):
                g.wait_ge(s_st.h, qb_st)
                g.collective_compute(
                    "AllGather", ALU.bypass, replica_groups=[list(range(c.NCORE))],
                    ins=[self.qb_in[:].opt()], outs=[self.qb_out[:].opt()],
                ).then_inc(self.s_cc.h, 1)
            self.s_cc.v += 1

    def proj_head(self, blk, wdram, h, is_q):
        c, nc = self.c, self.nc
        s_ld, s_st, s_gd, s_t, s_v = self.s_ld, self.s_st, self.s_gd, self.s_t, self.s_v
        row = self.qrow if is_q else self.krow
        if True:
            loads = []

            @blk.sync
            def _(sy, h=h):
                for cj in range(c.CT):
                    loads.append(self.w_load(sy, wdram, cj, h))

            v_pre = s_v.v
            splits = [v_pre + 2 * (i + 1) for i in range(c.CT)]

            @blk.vector
            def _(v, h=h, v_pre=v_pre):
                cnt = v_pre
                for i in range(c.CT):
                    slot, lv = loads[i]
                    v.wait_ge(self.s_wl[slot].h, lv)
                    ws = self.wslots[:, 128 * slot:128 * slot + 128]
                    v.tensor_copy(self.w_hi[:, 128 * slot:128 * slot + 128], ws
                                  ).then_inc(s_v.h, 1)
                    cnt += 1
                    v.wait_ge(s_v.h, cnt)
                    v.scalar_tensor_tensor(out=self.w_lo[:, 128 * slot:128 * slot + 128],
                                           in0=self.w_hi[:, 128 * slot:128 * slot + 128],
                                           scalar=-1.0, in1=ws,
                                           op0=ALU.mult, op1=ALU.add).then_inc(s_v.h, 1)
                    cnt += 1
            s_v.v += 2 * c.CT
            mm_gate = self.head_evac[h - 2] if h >= 2 else self.head_gate0

            @blk.tensor
            def _(t, h=h, mm_gate=mm_gate):
                t.wait_ge(s_v.h, mm_gate)
                psA = self.psA if h % 2 == 0 else self.psB
                psA2 = self.psA2 if h % 2 == 0 else self.psB2
                for cj in range(c.CT):
                    slot, lv = loads[cj]
                    t.wait_ge(s_v.h, splits[cj])
                    whi = self.w_hi[:, 128 * slot:128 * slot + 128]
                    wlo = self.w_lo[:, 128 * slot:128 * slot + 128]
                    sls = [slice(cj * c.TPC + ch * 512, cj * c.TPC + ch * 512 + c.CHW)
                           for ch in range(c.NCH)]
                    pas = [psA if ch == 0 else psA2 for ch in range(c.NCH)]
                    for ch in range(c.NCH):
                        t.matmul(pas[ch][:, 0:c.CHW], whi, self.z_hi[:, sls[ch]],
                                 start=(cj == 0), stop=False)
                        t.matmul(pas[ch][:, 0:c.CHW], whi, self.z_lo[:, sls[ch]],
                                 start=False, stop=False)
                    for ch in range(c.NCH):
                        inst = t.matmul(pas[ch][:, 0:c.CHW], wlo, self.z_hi[:, sls[ch]],
                                        start=False, stop=(cj == c.CT - 1))
                    inst.then_inc(self.s_wf.h, 1)
            wf_done = self.wload

            @blk.vector
            def _(v, h=h, wf_done=wf_done):
                v.wait_ge(self.s_wf.h, wf_done)
                v.wait_ge(s_gd.h, s_gd.v)
                v.wait_ge(s_st.h, s_st.v)
                psA = self.psA if h % 2 == 0 else self.psB
                psA2 = self.psA2 if h % 2 == 0 else self.psB2
                for ch in range(c.NCH):
                    pa = psA if ch == 0 else psA2
                    inst = v.tensor_copy(row[:, ch * 512:ch * 512 + c.CHW], pa[:, 0:c.CHW])
                inst.then_inc(s_v.h, 1)
            s_v.v += 1
            self.head_evac.append(s_v.v)

            @blk.gpsimd
            def _(g):
                g.wait_ge(s_v.h, s_v.v)
                g.dma_start(out=self.rope_a[0:64, :], in_=row[64:128, :]).then_inc(s_gd.h, 16)
                g.dma_start(out=self.rope_a[64:128, :], in_=row[0:64, :]).then_inc(s_gd.h, 16)
            s_gd.v += 32

            v_pre = s_v.v

            @blk.vector
            def _(v, h=h, v_pre=v_pre):
                v.wait_ge(s_gd.h, s_gd.v)
                v.tensor_tensor(out=self.rope_b[:, :], in0=row[:, :],
                                in1=self.cos_t[:, :], op=ALU.mult)
                v.tensor_tensor(out=self.rope_a[:, :], in0=self.rope_a[:, :],
                                in1=self.sin_t[:, :], op=ALU.mult).then_inc(s_v.h, 1)
                v.wait_ge(s_v.h, v_pre + 1)
                v.tensor_add(row[:, :], self.rope_b[:, :], self.rope_a[:, :]
                             ).then_inc(s_v.h, 1)
                if is_q:
                    v.wait_ge(s_v.h, v_pre + 2)
                    v.tensor_tensor(out=self.rope_b[:, :], in0=row[:, :],
                                    in1=self.tmask_b[:, :], op=ALU.mult
                                    ).then_inc(s_v.h, 1)
                    v.wait_ge(s_v.h, v_pre + 3)
                    v.reduce_sum(self.qbar[:, h:h + 1], self.rope_b[:, :],
                                 axis=AX.X).then_inc(s_v.h, 1)
            s_v.v += 4 if is_q else 2

            @blk.scalar
            def _(s, h=h):
                s.wait_ge(s_v.h, s_v.v)
                if is_q:
                    s.dma_start(out=ap2(self.q_dr, 128 * h * c.TPC, (c.TPC, 128), (1, c.TPC)),
                                in_=row[:, :]).then_inc(s_st.h, 16)
                else:
                    s.dma_start(out=ap2(self.kv_in, 128 * h * c.TPC, (c.TPC, 128), (1, c.TPC)),
                                in_=row[:, :]).then_inc(s_st.h, 16)
            s_st.v += 16

    def v_proj(self, blk):
        c, nc = self.c, self.nc
        s_ld, s_st, s_t, s_v = self.s_ld, self.s_st, self.s_t, self.s_v
        pss = [self.psA, self.psA2, self.psB, self.psB2]
        t_base, v_base, st_base = s_t.v, s_v.v, s_st.v
        vevac = []
        grp = 0
        for k in range(c.NSLOT):
            for fc in range(c.NFC):
                if True:
                    loads = []
                    mygrp = grp
                    grp += 1

                    @blk.sync
                    def _(sy, k=k, fc=fc):
                        for cj in range(c.CT):
                            i = self.wload
                            assert i % 4 == 0
                            b = i - self.NWS + 3
                            if b >= 0 and self.unit_rel[b] > 0:
                                sy.wait_ge(self.s_wf.h, self.unit_rel[b])
                            slot = i % 8
                            sy.dma_start(
                                out=self.wslots[:, 128 * slot:128 * slot + 512],
                                in_=ap2(self.wv, 128 * cj * c.D + 512 * fc, (c.D, 128), (1, 512))
                                ).then_inc(self.s_wl[slot].h, 16)
                            self.wl_cnt[slot] += 16
                            loads.append((slot, self.wl_cnt[slot]))
                            self.unit_rel.extend([i + 4] * 4)
                            self.wload += 4

                    wf_done = [0]
                    v_pre2 = s_v.v
                    splits = [v_pre2 + 2 * (i + 1) for i in range(c.CT)]

                    @blk.vector
                    def _(v, k=k, fc=fc, v_pre2=v_pre2):
                        cnt = v_pre2
                        for i in range(c.CT):
                            slot, lv = loads[i]
                            v.wait_ge(self.s_wl[slot].h, lv)
                            ws = self.wslots[:, 128 * slot:128 * slot + 512]
                            v.tensor_copy(self.w_hi[:, 128 * slot:128 * slot + 512], ws
                                          ).then_inc(s_v.h, 1)
                            cnt += 1
                            v.wait_ge(s_v.h, cnt)
                            v.scalar_tensor_tensor(
                                out=self.w_lo[:, 128 * slot:128 * slot + 512],
                                in0=self.w_hi[:, 128 * slot:128 * slot + 512],
                                scalar=-1.0, in1=ws,
                                op0=ALU.mult, op1=ALU.add).then_inc(s_v.h, 1)
                            cnt += 1
                    s_v.v += 2 * c.CT

                    @blk.tensor
                    def _(t, k=k, fc=fc, mygrp=mygrp):
                        pa = pss[mygrp % 4]
                        if mygrp >= 4:
                            t.wait_ge(s_v.h, vevac[mygrp - 4])
                        for cj in range(c.CT):
                            slot, lv = loads[cj]
                            t.wait_ge(s_v.h, splits[cj])
                            zhi = self.z_hi[:, cj * c.TPC + 128 * k:cj * c.TPC + 128 * k + 128]
                            zlo = self.z_lo[:, cj * c.TPC + 128 * k:cj * c.TPC + 128 * k + 128]
                            whi = self.w_hi[:, 128 * slot:128 * slot + 512]
                            wlo = self.w_lo[:, 128 * slot:128 * slot + 512]
                            t.matmul(pa[:, 0:512], zhi, whi, start=(cj == 0), stop=False)
                            t.matmul(pa[:, 0:512], zhi, wlo, start=False, stop=False)
                            t.matmul(pa[:, 0:512], zlo, whi,
                                     start=False, stop=(cj == c.CT - 1)
                                     ).then_inc(self.s_wf.h, 4)
                        wf_done[0] = self.wload

                    @blk.vector
                    def _(v, k=k, fc=fc, mygrp=mygrp):
                        pa = pss[mygrp % 4]
                        v.wait_ge(self.s_wf.h, wf_done[0])
                        if mygrp >= 2:
                            v.wait_ge(s_st.h, st_base + 16 * (mygrp - 1))
                        v.tensor_copy(self.ffn_buf[:, 512 * (mygrp % 2):512 * (mygrp % 2) + 512],
                                      pa[:, 0:512]).then_inc(s_v.h, 1)
                    s_v.v += 1
                    vevac.append(s_v.v)

                    @blk.scalar
                    def _(s, k=k, fc=fc, mygrp=mygrp):
                        s.wait_ge(s_v.h, vevac[mygrp])
                        s.dma_start(
                            out=ap2(self.kv_in, c.VOFF + 128 * k * c.D + 512 * fc,
                                    (c.D, 128), (1, 512)),
                            in_=self.ffn_buf[:, 512 * (mygrp % 2):512 * (mygrp % 2) + 512]
                            ).then_inc(s_st.h, 16)
        n = c.NSLOT * c.NFC
        s_st.v += 16 * n

    def p3_topk(self):
        c, nc = self.c, self.nc
        s_ld, s_t, s_v, s_g, s_gd = self.s_ld, self.s_t, self.s_v, self.s_g, self.s_gd
        with nc.Block() as blk:
            ld0 = s_ld.v

            @blk.sync
            def _(sy, ld0=ld0):
                sy.wait_ge(self.s_cc.h, 2)
                sy.dma_start(out=self.qb_all[:, :],
                             in_=ap3(self.qb_out, 0, (c.H, 128), (128 * c.H, c.NCORE), (1, c.H))
                             ).then_inc(s_ld.h, 16)
            s_ld.v += 16

            v_pre = s_v.v

            @blk.vector
            def _(v, ld0=ld0, v_pre=v_pre):
                v.wait_ge(s_ld.h, ld0 + 16)
                cnt = v_pre
                v.tensor_copy(self.qbar[:, :], self.qb_all[:, 0:c.H]).then_inc(s_v.h, 1)
                cnt += 1
                for r in range(1, c.NCORE):
                    v.wait_ge(s_v.h, cnt)
                    v.tensor_add(self.qbar[:, :], self.qbar[:, :],
                                 self.qb_all[:, r * c.H:(r + 1) * c.H]).then_inc(s_v.h, 1)
                    cnt += 1
            s_v.v += c.NCORE

            t_base = self.s_t.v
            for h in range(c.H):
                @blk.sync
                def _(sy, h=h):
                    if h > 0:
                        sy.wait_ge(self.s_t.h, t_base + h)
                    sy.dma_start(
                        out=ap3(self.kvis, 0, ((c.NVC + 1) * 128, 128), (128, c.NVC + 1), (1, 128)),
                        in_=ap3(self.kv_out, 128 * h * c.TPC,
                                (c.TPC, 128), (c.KVN, c.NVC + 1), (1, 128))
                        ).then_inc(self.s_kvi.h, 16)
                    self.s_kvi.v += 16

                @blk.tensor
                def _(t, h=h):
                    t.wait_ge(self.s_kvi.h, self.s_kvi.v)
                    t.wait_ge(s_v.h, s_v.v)
                    t.matmul(self.psM[0:1, 0:c.V0], self.qbar[:, h:h + 1],
                             self.kvis[:, c.VIS_START:c.VIS_START + c.V0],
                             start=(h == 0), stop=(h == c.H - 1)).then_inc(self.s_t.h, 1)
            self.s_t.v += c.H

            @blk.vector
            def _(v):
                v.wait_ge(s_t.h, s_t.v)
                v.tensor_copy(self.sc1[0:1, 0:c.V0], self.psM[0:1, 0:c.V0]
                              ).then_inc(s_v.h, 1)
            s_v.v += 1

            t_pre = s_t.v

            @blk.tensor
            def _(t, t_pre=t_pre):
                t.wait_ge(s_v.h, s_v.v)
                t.matmul(self.psT1[:, 0:c.V0], self.ones[0:1, :], self.sc1[0:1, 0:c.V0],
                         start=True, stop=True).then_inc(s_t.h, 1)
                for cc in range(c.NVC):
                    t.transpose(self.psT2[:, cc:cc + 1],
                                self.sc1[0:1, 128 * cc:128 * cc + 128],
                                self.iden[0:1, 0:1]).then_inc(s_t.h, 1)
            s_t.v += 1 + c.NVC

            g_pre = s_g.v

            @blk.gpsimd
            def _(g, g_pre=g_pre):
                g.iota(self.uio[:, :], [[1, c.V0]], base=0, channel_multiplier=0,
                       allow_small_or_imprecise_dtypes=True)
                for cc in range(c.NVC):
                    g.iota(self.vcol[:, cc:cc + 1], [[0, 1]], base=128 * cc,
                           channel_multiplier=1, allow_small_or_imprecise_dtypes=True)
                g.iota(self.colv[0:1, :], [[1, c.S]], base=0, channel_multiplier=0,
                       allow_small_or_imprecise_dtypes=True).then_inc(s_g.h, 1)
            s_g.v += 1

            v_pre = s_v.v

            @blk.vector
            def _(v, v_pre=v_pre):
                v.wait_ge(s_t.h, s_t.v)
                v.wait_ge(s_g.h, s_g.v)
                cnt = v_pre

                def step(op):
                    nonlocal cnt
                    op().then_inc(s_v.h, 1)
                    cnt += 1
                    v.wait_ge(s_v.h, cnt)
                step(lambda: v.tensor_copy(self.impb[:, :], self.psT1[:, 0:c.V0]))
                step(lambda: v.tensor_copy(self.impc[:, :], self.psT2[:, 0:c.NVC]))
                for cc in range(c.NVC):
                    step(lambda cc=cc: v.tensor_scalar(
                        out=self.g2_scr[:, :], in0=self.impb[:, :],
                        scalar1=self.impc[:, cc:cc + 1], scalar2=None,
                        op0=ALU.is_gt, accum_out=self.rankb[:, cc:cc + 1], op1=ALU.add))
                    step(lambda cc=cc: v.tensor_scalar(
                        out=self.g_scr[:, :], in0=self.impb[:, :],
                        scalar1=self.impc[:, cc:cc + 1], scalar2=None, op0=ALU.is_equal))
                    step(lambda cc=cc: v.scalar_tensor_tensor(
                        out=self.g2_scr[:, :], in0=self.uio[:, :],
                        scalar=self.vcol[:, cc:cc + 1], in1=self.g_scr[:, :],
                        op0=ALU.is_lt, op1=ALU.mult, accum_out=self.tieb[:, cc:cc + 1]))
                    step(lambda cc=cc: v.tensor_add(
                        self.rankb[:, cc:cc + 1], self.rankb[:, cc:cc + 1],
                        self.tieb[:, cc:cc + 1]))
                    step(lambda cc=cc: v.tensor_scalar(
                        out=self.keepc[:, cc:cc + 1], in0=self.rankb[:, cc:cc + 1],
                        scalar1=float(c.keep_now), scalar2=None, op0=ALU.is_lt))
            s_v.v = v_pre + 2 + 5 * c.NVC

            @blk.tensor
            def _(t):
                t.wait_ge(s_v.h, s_v.v)
                t.transpose(self.psT2[0:c.NVC, 0:128], self.keepc[:, 0:c.NVC],
                            self.iden[:, :]).then_inc(s_t.h, 1)
            s_t.v += 1

            @blk.vector
            def _(v):
                v.wait_ge(s_t.h, s_t.v)
                v.tensor_copy(self.keeprow[0:c.NVC, 0:128], self.psT2[0:c.NVC, 0:128]
                              ).then_inc(s_v.h, 1)
            s_v.v += 1

            @blk.gpsimd
            def _(g):
                g.wait_ge(s_v.h, s_v.v)
                for cc in range(c.NVC):
                    g.dma_start(out=self.sc3[0:1, 128 * cc:128 * cc + 128],
                                in_=self.keeprow[cc:cc + 1, 0:128]).then_inc(s_gd.h, 16)
            s_gd.v += 16 * c.NVC

            v_pre = s_v.v

            @blk.vector
            def _(v, v_pre=v_pre):
                v.wait_ge(s_gd.h, s_gd.v)
                v.wait_ge(s_g.h, s_g.v)
                v.tensor_scalar(out=self.sc3[0:1, 0:c.V0], in0=self.sc3[0:1, 0:c.V0],
                                scalar1=-1e9, scalar2=1e9, op0=ALU.mult, op1=ALU.add
                                ).then_inc(s_v.h, 1)
                v.wait_ge(s_v.h, v_pre + 1)
                v.tensor_add(self.colv[0:1, c.VIS_START:c.VIS_START + c.V0],
                             self.colv[0:1, c.VIS_START:c.VIS_START + c.V0],
                             self.sc3[0:1, 0:c.V0]).then_inc(s_v.h, 1)
            s_v.v += 2

            @blk.sync
            def _(sy):
                sy.wait_ge(s_v.h, s_v.v)
                sy.wait_ge(s_t.h, s_t.v)
                sy.dma_start(out=self.sc1[:, 0:c.NSLOT],
                             in_=ap2(self.tstart_d, 0, (c.NSLOT, 1), (1, c.NSLOT))
                             ).then_inc(s_ld.h, 16)
            s_ld.v += 16

            mspecs = [(kk, kk, None) for kk in range(min(2, c.NJC))]
            for s in range(c.NSLOT):
                for d in range(2):
                    kk = 2 * s + d
                    if kk < c.NJC:
                        mspecs.append((2 + 2 * s + d, kk, s))
            t_pre = s_t.v
            v_pre = s_v.v

            @blk.tensor
            def _(t, t_pre=t_pre, v_pre=v_pre):
                t.wait_ge(s_v.h, v_pre)
                t.wait_ge(s_ld.h, s_ld.v)
                t.matmul(self.psT2[:, 0:c.NSLOT], self.ones[0:1, :],
                         self.sc1[0:1, 0:c.NSLOT], start=True, stop=True).then_inc(s_t.h, 1)
                for n, (idx, kk, s) in enumerate(mspecs):
                    pa = self.psT1 if n % 2 == 0 else self.psM
                    t.wait_ge(s_v.h, max(0, v_pre + 1 + c.NSLOT + n - 1))
                    t.matmul(pa[:, 0:512], self.ones[0:1, :],
                             self.colv[0:1, 512 * kk:512 * kk + 512],
                             start=True, stop=True).then_inc(s_t.h, 1)

            @blk.vector
            def _(v, t_pre=t_pre, v_pre=v_pre):
                cnt = v_pre
                v.wait_ge(s_t.h, t_pre + 1)
                v.tensor_copy(self.tst_b[:, :], self.psT2[:, 0:c.NSLOT]).then_inc(s_v.h, 1)
                cnt += 1
                for s in range(c.NSLOT):
                    v.wait_ge(s_v.h, cnt)
                    v.tensor_scalar(out=self.iabs[:, s:s + 1], in0=self.piota[:, :],
                                    scalar1=self.tst_b[:, s:s + 1], scalar2=None,
                                    op0=ALU.add).then_inc(s_v.h, 1)
                    cnt += 1
                for n, (idx, kk, s) in enumerate(mspecs):
                    pa = self.psT1 if n % 2 == 0 else self.psM
                    v.wait_ge(s_t.h, t_pre + 1 + n + 1)
                    if s is None:
                        v.tensor_scalar(
                            out=self.maskt[:, 512 * idx:512 * idx + 512],
                            in0=pa[:, 0:512],
                            scalar1=float(c.S) + 0.5, scalar2=NEG_BIG,
                            op0=ALU.is_gt, op1=ALU.mult).then_inc(s_v.h, 1)
                    else:
                        v.tensor_scalar(
                            out=self.maskt[:, 512 * idx:512 * idx + 512],
                            in0=pa[:, 0:512],
                            scalar1=self.iabs[:, s:s + 1], scalar2=NEG_BIG,
                            op0=ALU.is_gt, op1=ALU.mult).then_inc(s_v.h, 1)
                    cnt += 1
            s_t.v += 1 + len(mspecs)
            s_v.v += 1 + c.NSLOT + len(mspecs)

    def p4_attn(self):
        c, nc = self.c, self.nc
        s_ld, s_st, s_t, s_v, s_a = self.s_ld, self.s_st, self.s_t, self.s_v, self.s_a
        scale = 1.0 / math.sqrt(128.0)
        ngrp = c.S // 1024
        SC = c.S
        with nc.Block() as blk:
            for h in range(c.H):
                ld0 = s_ld.v
                t_head_pre = s_t.v

                @blk.sync
                def _(sy, h=h, t_head_pre=t_head_pre):
                    if h > 0:
                        sy.wait_ge(s_t.h, t_head_pre)
                    for sg in range(ngrp):
                        sy.dma_start(
                            out=ap3(self.kh, 1024 * sg, (c.S, 128), (128, 8), (1, 128)),
                            in_=ap3(self.kv_out, 128 * h * c.TPC + 128 * sg,
                                    (c.TPC, 128), (c.KVN, 8), (1, 128))
                            ).then_inc(s_ld.h, 16)
                        sy.dma_start(
                            out=ap3(self.vh, 1024 * sg, (c.S, 128), (128, 8), (1, 128)),
                            in_=ap3(self.kv_out, c.VOFF + 128 * sg * c.D + 128 * h,
                                    (c.D, 128), (c.KVN, 8), (1, 128))
                            ).then_inc(s_ld.h, 16)
                    sy.dma_start(out=self.qh[:, :],
                                 in_=ap2(self.q_dr, 128 * h * c.TPC, (c.TPC, 128), (1, c.TPC))
                                 ).then_inc(s_ld.h, 16)
                s_ld.v += 16 * (2 * ngrp + 1)
                ld_end = s_ld.v

                tpv_end = []
                stt_end = []
                norm_end = []

                def stage_a(s, h=h, ld_end=ld_end, tpv_end=tpv_end, stt_end=stt_end):
                    nch = c.chunks_per_slot[s]
                    par = s % 2
                    t_pre, v_pre, a_pre = s_t.v, s_v.v, s_a.v
                    def mask_idx(kk, s=s):
                        if kk >= 2 * s:
                            return 2 + 2 * s + (kk - 2 * s)
                        if kk < 2:
                            return kk
                        return None

                    def evac_pos(kk, v_pre=v_pre, mask_idx=mask_idx):
                        p = v_pre
                        for k2 in range(kk + 1):
                            p += 1 if mask_idx(k2) is None else 2
                        return p
                    nmsk_all = sum(1 for kk in range(nch) if mask_idx(kk) is not None)
                    stt_end.append(v_pre + nch + nmsk_all)
                    prev_gate = stt_end[s - 1] if s >= 1 else 0

                    @blk.tensor
                    def _(t, s=s, nch=nch, v_pre=v_pre, ld_end=ld_end, prev_gate=prev_gate,
                          evac_pos=evac_pos):
                        t.wait_ge(s_ld.h, ld_end)
                        for kk in range(nch):
                            pa = self.psA if kk % 2 == 0 else self.psA2
                            if kk >= 2:
                                t.wait_ge(s_v.h, evac_pos(kk - 2))
                            elif prev_gate > 0:
                                t.wait_ge(s_v.h, prev_gate)
                            t.matmul(pa[:, 0:512], self.qh[:, 128 * s:128 * s + 128],
                                     self.kh[:, 512 * kk:512 * kk + 512],
                                     start=True, stop=True).then_inc(s_t.h, 1)

                    @blk.vector
                    def _(v, s=s, nch=nch, par=par, t_pre=t_pre, v_pre=v_pre, a_pre=a_pre):
                        v.wait_ge(s_a.h, a_pre)
                        if s >= 2:
                            v.wait_ge(s_t.h, tpv_end[s - 2])
                        cnt = v_pre
                        for kk in range(nch):
                            pa = self.psA if kk % 2 == 0 else self.psA2
                            v.wait_ge(s_t.h, t_pre + kk + 1)
                            mi = mask_idx(kk)
                            sl = self.scores[:, SC * par + 512 * kk:SC * par + 512 * kk + 512]
                            cm = self.cmax[:, c.NJC * par + kk:c.NJC * par + kk + 1]
                            if mi is None:
                                v.tensor_scalar(
                                    out=sl, in0=pa[:, 0:512], scalar1=scale, scalar2=None,
                                    op0=ALU.mult, op1=ALU.max,
                                    accum_out=cm).then_inc(s_v.h, 1)
                                cnt += 1
                            else:
                                v.scalar_tensor_tensor(
                                    out=sl, in0=pa[:, 0:512], scalar=scale,
                                    in1=self.maskt[:, 512 * mi:512 * mi + 512],
                                    op0=ALU.mult, op1=ALU.add).then_inc(s_v.h, 1)
                                cnt += 1
                                v.wait_ge(s_v.h, cnt)
                                v.reduce_max(cm, sl, axis=AX.X).then_inc(s_v.h, 1)
                                cnt += 1
                        v.wait_ge(s_v.h, cnt)
                        v.reduce_max(self.nmax[:, par:par + 1],
                                     self.cmax[:, c.NJC * par:c.NJC * par + nch],
                                     axis=AX.X, negate=True).then_inc(s_v.h, 1)
                    s_t.v += nch
                    nmsk = sum(1 for kk in range(nch) if mask_idx(kk) is not None)
                    s_v.v += nch + nmsk + 1

                    @blk.scalar
                    def _(sc, s=s, nch=nch, par=par):
                        sc.wait_ge(s_v.h, s_v.v)
                        sc.activation(self.scores[:, SC * par:SC * par + 512 * nch],
                                      self.scores[:, SC * par:SC * par + 512 * nch],
                                      AF.Exp, bias=self.nmax[:, par:par + 1], scale=1.0,
                                      accum_out=self.den[:, 4 * par:4 * par + 1]
                                      ).then_inc(s_a.h, 1)
                    s_a.v += 1

                    v_pre = s_v.v

                    @blk.vector
                    def _(v, s=s, par=par, v_pre=v_pre):
                        v.wait_ge(s_a.h, s_a.v)
                        cnt = v_pre
                        d0 = 4 * par

                        def step(op):
                            nonlocal cnt
                            op().then_inc(s_v.h, 1)
                            cnt += 1
                            v.wait_ge(s_v.h, cnt)
                        step(lambda: v.reciprocal(self.den[:, d0 + 1:d0 + 2],
                                                  self.den[:, d0:d0 + 1]))
                        step(lambda: v.tensor_tensor(out=self.den[:, d0 + 2:d0 + 3],
                                                     in0=self.den[:, d0:d0 + 1],
                                                     in1=self.den[:, d0 + 1:d0 + 2], op=ALU.mult))
                        step(lambda: v.tensor_scalar(out=self.den[:, d0 + 2:d0 + 3],
                                                     in0=self.den[:, d0 + 2:d0 + 3],
                                                     scalar1=-1.0, scalar2=2.0,
                                                     op0=ALU.mult, op1=ALU.add))
                        step(lambda: v.tensor_tensor(out=self.den[:, d0 + 3:d0 + 4],
                                                     in0=self.den[:, d0 + 1:d0 + 2],
                                                     in1=self.den[:, d0 + 2:d0 + 3], op=ALU.mult))
                    s_v.v = v_pre + 4

                    @blk.scalar
                    def _(sc, s=s, nch=nch, par=par):
                        sc.wait_ge(s_v.h, s_v.v)
                        sc.activation(self.scores[:, SC * par:SC * par + 512 * nch],
                                      self.scores[:, SC * par:SC * par + 512 * nch],
                                      AF.Identity, bias=0.0,
                                      scale=self.den[:, 4 * par + 3:4 * par + 4]
                                      ).then_inc(s_a.h, 1)
                    s_a.v += 1
                    norm_end.append(s_a.v)

                def stage_b(s, h=h, tpv_end=tpv_end):
                    nch = c.chunks_per_slot[s]
                    par = s % 2
                    njt = 4 * nch
                    t_pre, v_pre = s_t.v, s_v.v
                    a_pre = norm_end[s]
                    order = []
                    for jj in range(njt):
                        order.append(("T", jj))
                        if jj >= 1:
                            order.append(("P", jj - 1))
                    order.append(("P", njt - 1))
                    tpos = {}
                    for idx, (kind, jj) in enumerate(order):
                        tpos[(kind, jj)] = t_pre + idx + 1
                    vpos = {jj: v_pre + jj + 1 for jj in range(njt)}

                    @blk.tensor
                    def _(t, s=s, njt=njt, par=par, order=order, tpos=tpos, vpos=vpos,
                          a_pre=a_pre):
                        t.wait_ge(s_a.h, a_pre)
                        for kind, jj in order:
                            if kind == "T":
                                pst = self.psT1 if jj % 2 == 0 else self.psT2
                                if jj >= 2:
                                    t.wait_ge(s_v.h, vpos[jj - 2])
                                t.transpose(pst[:, 0:128],
                                            self.scores[:, SC * par + 128 * jj:SC * par + 128 * jj + 128],
                                            self.iden[:, :]).then_inc(s_t.h, 1)
                            else:
                                t.wait_ge(s_v.h, vpos[jj])
                                t.matmul(self.psAt[:, 0:128],
                                         self.vh[:, 128 * jj:128 * jj + 128],
                                         self.ptbuf[:, 128 * (jj % 2):128 * (jj % 2) + 128],
                                         start=(jj == 0), stop=(jj == njt - 1)
                                         ).then_inc(s_t.h, 1)

                    @blk.vector
                    def _(v, s=s, njt=njt, tpos=tpos):
                        for jj in range(njt):
                            v.wait_ge(s_t.h, tpos[("T", jj)])
                            if jj >= 2:
                                v.wait_ge(s_t.h, tpos[("P", jj - 2)])
                            v.tensor_copy(self.ptbuf[:, 128 * (jj % 2):128 * (jj % 2) + 128],
                                          (self.psT1 if jj % 2 == 0 else self.psT2)[:, 0:128]
                                          ).then_inc(s_v.h, 1)
                    s_t.v += 2 * njt
                    s_v.v += njt
                    tpv_end.append(s_t.v)

                    @blk.vector
                    def _(v, s=s):
                        v.wait_ge(s_t.h, s_t.v)
                        if s == 0:
                            v.wait_ge(s_st.h, s_st.v)
                        v.tensor_copy(self.aout[:, 128 * s:128 * s + 128], self.psAt[:, 0:128]
                                      ).then_inc(s_v.h, 1)
                    s_v.v += 1

                stage_a(0)
                for s in range(1, c.NSLOT):
                    stage_a(s)
                    stage_b(s - 1)
                stage_b(c.NSLOT - 1)

                @blk.scalar
                def _(sc, h=h):
                    sc.wait_ge(s_v.h, s_v.v)
                    sc.dma_start(out=ap2(self.at_dr, 128 * h * c.TPC, (c.TPC, 128), (1, c.TPC)),
                                 in_=self.aout[:, :]).then_inc(s_st.h, 16)
                s_st.v += 16

    def p5_oproj(self):
        c, nc = self.c, self.nc
        s_ld, s_st, s_v = self.s_ld, self.s_st, self.s_v
        with nc.Block() as blk:
            st0 = s_st.v
            v_pre = s_v.v
            splits = []

            @blk.sync
            def _(sy, st0=st0, v_pre=v_pre):
                sy.wait_ge(s_st.h, st0)
                for cj in range(c.CT):
                    if cj >= 2:
                        sy.wait_ge(s_v.h, v_pre + 2 * (cj - 1))
                    sy.dma_start(out=self.xt_buf[:, c.TPC * (cj % 2):c.TPC * (cj % 2) + c.TPC],
                                 in_=ap2(self.at_dr, 128 * cj * c.TPC, (c.TPC, 128), (1, c.TPC))
                                 ).then_inc(self.s_x.h, 16)
                    self.s_x.v += 16
                    splits.append(self.s_x.v)
                for cj in range(c.CT):
                    splits[cj] = splits[min(cj | 1, c.CT - 1)]

            @blk.vector
            def _(v, v_pre=v_pre):
                cnt = v_pre
                for cj in range(c.CT):
                    v.wait_ge(self.s_x.h, splits[cj])
                    xs = self.xt_buf[:, c.TPC * (cj % 2):c.TPC * (cj % 2) + c.TPC]
                    zh = self.z_hi[:, cj * c.TPC:(cj + 1) * c.TPC]
                    zl = self.z_lo[:, cj * c.TPC:(cj + 1) * c.TPC]
                    v.tensor_copy(zh, xs).then_inc(s_v.h, 1)
                    cnt += 1
                    v.wait_ge(s_v.h, cnt)
                    v.scalar_tensor_tensor(out=zl, in0=zh, scalar=-1.0, in1=xs,
                                           op0=ALU.mult, op1=ALU.add).then_inc(s_v.h, 1)
                    cnt += 1
            s_v.v += 2 * c.CT
            self.proj_pass(blk, self.wo, self.x2_dr, self.x1_dr,
                           half_bias=False, lora=False)

    def p7_finish(self):
        with self.nc.Block() as blk:
            @blk.sync
            def _(sy):
                sy.wait_ge(self.s_st.h, self.s_st.v)
                sy.wait_ge(self.s_ld.h, self.s_ld.v)

            @blk.gpsimd
            def _(g):
                g.wait_ge(self.s_gd.h, self.s_gd.v)


def build_nc(cfg=None):
    return Prog(cfg or full_cfg()).build()


def make_in_maps(cfg, x_full, q_w, k_w, v_w, o_w, lora_A, lora_B, W, bias, lora_w):
    c = _C(cfg)
    x = np.ascontiguousarray(np.asarray(x_full, dtype=np.float32).reshape(c.S, c.D))
    xt = x.reshape(c.S // 128, 128, c.D)
    wq = np.ascontiguousarray(np.asarray(q_w, np.float32).T)
    wk = np.ascontiguousarray(np.asarray(k_w, np.float32).T)
    wv = np.ascontiguousarray(np.asarray(v_w, np.float32).T)
    wo = np.ascontiguousarray(np.asarray(o_w, np.float32).T)
    wf = np.ascontiguousarray(np.asarray(W, np.float32).T)
    acat = np.ascontiguousarray(np.asarray(lora_A, np.float32).reshape(128, c.D).T)
    bcat = np.ascontiguousarray(
        np.asarray(lora_B, np.float32).transpose(0, 2, 1).reshape(128, c.D))
    bias = np.ascontiguousarray(np.asarray(bias, np.float32))
    lw_rep = np.ascontiguousarray(np.repeat(np.asarray(lora_w, np.float32), c.RANK))
    in_maps = []
    for ci in range(8):
        shard = np.ascontiguousarray(xt[ci::8].reshape(c.TPC, c.D))
        tstart = (128.0 * (8 * np.arange(c.NSLOT) + ci)).astype(np.float32)
        tmask = np.zeros(c.TPC, np.float32)
        for s in range(c.NSLOT):
            t0 = 128 * (8 * s + ci)
            lo = max(c.INSTR_START, t0) - t0
            hi = min(c.instr_end, t0 + 128) - t0
            if hi > lo:
                tmask[128 * s + lo:128 * s + hi] = 1.0
        in_maps.append({
            "x": shard, "wqT": wq, "wkT": wk, "wvT": wv, "woT": wo, "WT": wf,
            "AcatT": acat, "BcatT": bcat, "bias": bias, "lw_rep": lw_rep,
            "tstart": tstart, "textmask": tmask,
        })
    return in_maps


def assemble_out(cfg, results):
    c = _C(cfg)
    out = np.empty((c.S // 128, 128, c.D), np.float32)
    for ci in range(8):
        out[ci::8] = np.asarray(results[ci]["y"]).reshape(c.NSLOT, 128, c.D)
    return out.reshape(1, c.S, c.D)


def kernel(L0_attn_hidden_states, L0_attn_q_w, L0_attn_k_w, L0_attn_v_w, L0_attn_o_w,
           L0_ffn_lora_A, L0_ffn_lora_B, L0_ffn_W, L0_ffn_bias, L0_ffn_lora_weights):
    cfg = full_cfg()
    in_maps = make_in_maps(cfg, L0_attn_hidden_states, L0_attn_q_w, L0_attn_k_w,
                           L0_attn_v_w, L0_attn_o_w, L0_ffn_lora_A, L0_ffn_lora_B,
                           L0_ffn_W, L0_ffn_bias, L0_ffn_lora_weights)
    nc = build_nc(cfg)
    res = run_bass_kernel_spmd(nc, in_maps, core_ids=list(range(8)))
    return assemble_out(cfg, res.results)


# version 17 (final)
# speedup vs baseline: 1.0269x; 1.0269x over previous
import math
import contextlib
import numpy as np

import concourse.bass as bass
import concourse.mybir as mybir
from concourse.bass_utils import run_bass_kernel_spmd

F32 = mybir.dt.float32
AF = mybir.ActivationFunctionType
ALU = mybir.AluOpType
AX = mybir.AxisListType

NEG_BIG = -1e30
FMAX_NEG = float(np.finfo(np.float32).min)


def full_cfg():
    return dict(S=8192, D=4096, H=32,
                VIS_START=1, VIS_END=513, INSTR_START=513, EXP_ACTIONS=56,
                KEEP_RATIO=0.5, ROPE_THETA=10000.0, N_LORA=8, RANK=16,
                LORA_ALPHA=32)


def mini_cfg():
    return dict(S=2048, D=512, H=4,
                VIS_START=1, VIS_END=257, INSTR_START=257, EXP_ACTIONS=56,
                KEEP_RATIO=0.5, ROPE_THETA=10000.0, N_LORA=8, RANK=16,
                LORA_ALPHA=32)


class _C:
    def __init__(self, cfg):
        self.__dict__.update(cfg)
        self.NCORE = 8
        self.Dh = 128
        self.HID = self.H * self.Dh
        assert self.HID == self.D
        self.TPC = self.S // self.NCORE
        self.NSLOT = self.TPC // 128
        self.CT = self.D // 128
        self.NCH = (self.TPC + 511) // 512
        self.CHW = min(512, self.TPC)
        self.NJC = self.S // 512
        self.NFC = self.D // 512
        self.instr_end = max(self.INSTR_START, self.S - self.EXP_ACTIONS)
        self.V0 = self.VIS_END - self.VIS_START
        assert self.V0 % 128 == 0 and self.V0 <= 512
        self.NVC = self.V0 // 128
        self.keep_now = max(1, min(self.V0, int(round(self.V0 * self.KEEP_RATIO))))
        assert self.N_LORA * self.RANK == 128
        self.LSCALE = float(self.LORA_ALPHA) / self.RANK
        self.KVN = self.HID * self.TPC + self.TPC * self.D
        self.VOFF = self.HID * self.TPC
        self.chunks_per_slot = [min(self.NJC, (128 * (8 * s + 8) + 511) // 512)
                                for s in range(self.NSLOT)]


class Sem:
    def __init__(self, st, nc, name):
        self.h = st.enter_context(nc.semaphore(name))
        self.v = 0


def ap2(t, off, d0, d1):
    return bass.AP(t, off, [list(d0), list(d1)])


def ap3(t, off, d0, d1, d2):
    return bass.AP(t, off, [list(d0), list(d1), list(d2)])


class Prog:
    def __init__(self, cfg):
        self.c = _C(cfg)
        self.nc = bass.Bass("TRN2", target_bir_lowering=False)
        self.st = contextlib.ExitStack()
        self.uid = 0

    def sem(self, name):
        return Sem(self.st, self.nc, name)

    def _n(self, base):
        self.uid += 1
        return f"{base}_{self.uid}"

    def build(self):
        c, nc, st = self.c, self.nc, self.st
        self.x_in = nc.dram_tensor("x", [c.TPC, c.D], F32, kind="ExternalInput")
        self.wq = nc.dram_tensor("wqT", [c.D, c.D], F32, kind="ExternalInput")
        self.wk = nc.dram_tensor("wkT", [c.D, c.D], F32, kind="ExternalInput")
        self.wv = nc.dram_tensor("wvT", [c.D, c.D], F32, kind="ExternalInput")
        self.wo = nc.dram_tensor("woT", [c.D, c.D], F32, kind="ExternalInput")
        self.wf = nc.dram_tensor("WT", [c.D, c.D], F32, kind="ExternalInput")
        self.acat = nc.dram_tensor("AcatT", [c.D, 128], F32, kind="ExternalInput")
        self.bcat = nc.dram_tensor("BcatT", [128, c.D], F32, kind="ExternalInput")
        self.bias_d = nc.dram_tensor("bias", [c.D], F32, kind="ExternalInput")
        self.lw_d = nc.dram_tensor("lw_rep", [128], F32, kind="ExternalInput")
        self.tstart_d = nc.dram_tensor("tstart", [c.NSLOT], F32, kind="ExternalInput")
        self.tmask_d = nc.dram_tensor("textmask", [c.TPC], F32, kind="ExternalInput")
        self.y_out = nc.dram_tensor("y", [c.TPC, c.D], F32, kind="ExternalOutput")
        self.x1_dr = nc.dram_tensor("x1_dr", [c.TPC, c.D], F32)
        self.x2_dr = nc.dram_tensor("x2_dr", [c.TPC, c.D], F32)
        self.q_dr = nc.dram_tensor("q_dr", [c.HID, c.TPC], F32)
        self.at_dr = nc.dram_tensor("at_dr", [c.HID, c.TPC], F32)
        self.kv_in = nc.dram_tensor("kv_in", [c.KVN], F32)
        self.kv_out = nc.dram_tensor("kv_out", [c.NCORE * c.KVN], F32)
        self.qb_in = nc.dram_tensor("qb_in", [128 * c.H], F32)
        self.qb_out = nc.dram_tensor("qb_out", [c.NCORE * 128 * c.H], F32)

        sb = lambda n, s: st.enter_context(nc.sbuf_tensor(n, s, F32))
        self.iden = sb("iden", [128, 128])
        self.ones = sb("ones", [1, 128])
        self.piota = sb("piota", [128, 1])
        self.bias_sb = sb("bias_sb", [128, c.CT])
        self.hbias = sb("hbias", [128, c.CT])
        self.coefs = sb("coefs", [128, 1])
        self.qbar = sb("qbar", [128, c.H])
        self.stats = sb("stats", [128, 24])
        self.sc1 = sb("sc1", [1, 512])
        self.wslots = sb("wslots", [128, 8 * 128])
        self.NWS = 8

        ps = lambda n, s: st.enter_context(nc.psum_tensor(n, s, F32))
        self.psA = ps("psA", [128, 512])
        self.psA2 = ps("psA2", [128, 512])
        self.psB = ps("psB", [128, 512])
        self.psB2 = ps("psB2", [128, 512])
        self.psT1 = ps("psT1", [128, 512])
        self.psT2 = ps("psT2", [128, 512])
        self.psAt = ps("psAt", [128, 512])
        self.psM = ps("psM", [128, 512])

        self.s_ld = self.sem("s_ld")
        self.s_x = self.sem("s_x")
        self.s_kv = self.sem("s_kv")
        self.s_zl = self.sem("s_zl")
        self.s_kvi = self.sem("s_kvi")
        self.s_ep = self.sem("s_ep")
        self.s_wl = [self.sem(f"s_wl{i}") for i in range(8)]
        self.wl_cnt = [0] * 8
        self.unit_rel = []
        self.s_st = self.sem("s_st")
        self.s_gd = self.sem("s_gd")
        self.s_t = self.sem("s_t")
        self.s_v = self.sem("s_v")
        self.s_a = self.sem("s_a")
        self.s_g = self.sem("s_g")
        self.s_wf = self.sem("s_wf")
        self.s_cc = self.sem("s_cc")
        self.s_bar = self.sem("s_bar")
        self.wload = 0

        with self.st:
            self.p0_setup()
            self.barrier()
            with contextlib.ExitStack() as g:
                self.alloc_proj(g, lora=True)
                self.ffn_phase(self.x_in, self.x_in, self.x1_dr)
                self.barrier()
            with contextlib.ExitStack() as g:
                self.alloc_proj(g, lora=False, qkv=True)
                self.p2_qkv()
                self.barrier()
            with contextlib.ExitStack() as g:
                self.alloc_span(g)
                with contextlib.ExitStack() as g3:
                    self.alloc_p3(g3)
                    self.p3_topk()
                    self.barrier()
                with contextlib.ExitStack() as g4:
                    self.alloc_p4(g4)
                    self.p4_attn()
                    self.barrier()
            with contextlib.ExitStack() as g:
                self.alloc_proj(g, lora=False)
                self.p5_oproj()
                self.barrier()
            with contextlib.ExitStack() as g:
                self.alloc_proj(g, lora=True)
                self.ffn_phase(self.x2_dr, self.x2_dr, self.y_out)
            self.p7_finish()
        return self.nc

    def alloc_proj(self, g, lora, qkv=False):
        c, nc = self.c, self.nc
        sb = lambda n, s: g.enter_context(nc.sbuf_tensor(self._n(n), s, F32))
        sbb = lambda n, s: g.enter_context(nc.sbuf_tensor(self._n(n), s, mybir.dt.bfloat16))
        self.z_hi = sbb("z_hi", [128, c.CT * c.TPC])
        self.z_lo = sbb("z_lo", [128, c.CT * c.TPC])
        self.w_hi = sbb("w_hi", [128, 8 * 128])
        self.w_lo = sbb("w_lo", [128, 8 * 128])
        self.xt_buf = sb("xt_buf", [128, c.D])
        self.zt_buf = sb("zt_buf", [128, c.D])
        self.ffn_buf = sb("ffn_buf", [128, 1024 if qkv else 2 * max(c.TPC, 512)])
        if not qkv:
            self.ep_x = sb("ep_x", [128, c.NSLOT * 128])
            self.ep_o = sb("ep_o", [128, c.NSLOT * 128])
        self.bn_buf = sb("bn_buf", [128, 6 * ((c.D + 511) // 512)])
        if lora:
            self.hsc_hi = sbb("hsc_hi", [128, c.TPC])
            self.hsc_lo = sbb("hsc_lo", [128, c.TPC])
        if qkv:
            self.cos_t = sb("cos_t", [128, c.TPC])
            self.sin_t = sb("sin_t", [128, c.TPC])
            self.tmask_b = sb("tmask_b", [128, c.TPC])
            self.qrow = sb("qrow", [128, c.TPC])
            self.krow = sb("krow", [128, c.TPC])
            self.rope_a = sb("rope_a", [128, c.TPC])
            self.rope_b = sb("rope_b", [128, c.TPC])

    def alloc_span(self, g):
        c, nc = self.c, self.nc
        sb = lambda n, s: g.enter_context(nc.sbuf_tensor(self._n(n), s, F32))
        self.maskt = sb("maskt", [128, (2 + 2 * c.NSLOT) * 512])
        self.iabs = sb("iabs", [128, c.NSLOT])
        self.tst_b = sb("tst_b", [128, c.NSLOT])

    def alloc_p3(self, g):
        c, nc = self.c, self.nc
        sb = lambda n, s: g.enter_context(nc.sbuf_tensor(self._n(n), s, F32))
        self.kvis = sb("kvis", [128, (c.NVC + 1) * 128])
        self.colv = sb("colv", [1, c.S])
        self.qb_all = sb("qb_all", [128, c.NCORE * c.H])
        self.impb = sb("impb", [128, c.V0])
        self.impc = sb("impc", [128, c.NVC])
        self.rankb = sb("rankb", [128, c.NVC])
        self.tieb = sb("tieb", [128, c.NVC])
        self.keepc = sb("keepc", [128, c.NVC])
        self.uio = sb("uio", [128, c.V0])
        self.vcol = sb("vcol", [128, c.NVC])
        self.g_scr = sb("g_scr", [128, c.V0])
        self.g2_scr = sb("g2_scr", [128, c.V0])
        self.keeprow = sb("keeprow", [c.NVC, 128])
        self.sc3 = sb("sc3", [1, 512])

    def alloc_p4(self, g):
        c, nc = self.c, self.nc
        sb = lambda n, s: g.enter_context(nc.sbuf_tensor(self._n(n), s, F32))
        self.kh = sb("kh", [128, c.S])
        self.vh = sb("vh", [128, c.S])
        self.qh = sb("qh", [128, c.TPC])
        self.scores = sb("scores", [128, 2 * c.S])
        self.aout = sb("aout", [128, c.TPC])
        self.ptbuf = sb("ptbuf", [128, 2 * 512])
        self.cmax = sb("cmax", [128, 2 * c.NJC])
        self.nmax = sb("nmax", [128, 2])
        self.den = sb("den", [128, 8])

    def barrier(self):
        bar = self.s_bar
        target = bar.v + 5
        with self.nc.Block() as blk:
            @blk.sync
            def _(sy):
                for s in (self.s_st, self.s_gd, self.s_ld, self.s_x, self.s_ep):
                    if s.v > 0:
                        sy.wait_ge(s.h, s.v)
                sy.sem_inc(bar.h, 1)
                sy.wait_ge(bar.h, target)
            for name in ("tensor", "vector", "scalar", "gpsimd"):
                def body(e, _t=target):
                    e.sem_inc(bar.h, 1)
                    e.wait_ge(bar.h, _t)
                getattr(blk, name)(body)
        bar.v = target

    def p0_setup(self):
        c, nc = self.c, self.nc
        s_ld, s_g, s_v = self.s_ld, self.s_g, self.s_v
        with nc.Block() as blk:
            ld0 = s_ld.v

            @blk.sync
            def _(sy):
                sy.dma_start(out=self.bias_sb[:, :],
                             in_=ap2(self.bias_d, 0, (1, 128), (128, c.CT)),
                             allow_slow_non_contiguous=True).then_inc(s_ld.h, 16)
                sy.dma_start(out=self.coefs[:, :],
                             in_=ap2(self.lw_d, 0, (1, 128), (1, 1))).then_inc(s_ld.h, 16)
            s_ld.v += 32

            @blk.gpsimd
            def _(g):
                g.memset(self.iden[:, :], 0.0)
                g.memset(self.ones[:, :], 1.0)
                g.iota(self.piota[:, :], [[0, 1]], base=0, channel_multiplier=1,
                       allow_small_or_imprecise_dtypes=True).then_inc(s_g.h, 1)
                g.wait_ge(s_g.h, s_g.v + 1)
                g.affine_select(out=self.iden[:, :], in_=self.iden[:, :],
                                compare_op=ALU.not_equal, fill=1.0, base=0,
                                pattern=[[-1, 128]], channel_multiplier=1
                                ).then_inc(s_g.h, 1)
            s_g.v += 2

            @blk.vector
            def _(v, ld0=ld0):
                v.wait_ge(s_ld.h, ld0 + 32)
                v.tensor_scalar(out=self.hbias[:, :], in0=self.bias_sb[:, :],
                                scalar1=0.5, scalar2=None, op0=ALU.mult)
                v.tensor_scalar(out=self.coefs[:, :], in0=self.coefs[:, :],
                                scalar1=float(c.LSCALE), scalar2=None, op0=ALU.mult
                                ).then_inc(s_v.h, 1)
            s_v.v += 1

    def z_build(self, blk, src_dram):
        c, nc = self.c, self.nc
        s_ld, s_t, s_v, s_a = self.s_ld, self.s_t, self.s_v, self.s_a
        nbn = (c.D + 511) // 512
        for k in range(c.NSLOT):
            if True:
                ld0 = self.s_x.v
                a_pre = s_a.v

                @blk.sync
                def _(sy, k=k, a_pre=a_pre):
                    if k > 0:
                        sy.wait_ge(self.s_a.h, a_pre)
                    sy.dma_start(out=self.xt_buf[:, :],
                                 in_=self.x_like(src_dram, k)).then_inc(self.s_x.h, 16)
                self.s_x.v += 16

                v_pre = s_v.v

                @blk.vector
                def _(v, ld0=ld0, v_pre=v_pre):
                    v.wait_ge(self.s_x.h, ld0 + 16)
                    cnt = v_pre

                    def step(op):
                        nonlocal cnt
                        op().then_inc(s_v.h, 1)
                        cnt += 1
                        v.wait_ge(s_v.h, cnt)
                    for i in range(nbn):
                        w = min(512, c.D - 512 * i)
                        inst = v.bn_stats(self.bn_buf[:, 6 * i:6 * i + 6],
                                          self.xt_buf[:, 512 * i:512 * i + w])
                    inst.then_inc(s_v.h, 1)
                    cnt += 1
                    v.wait_ge(s_v.h, cnt)
                    step(lambda: v.bn_aggr(self.stats[:, 0:2], self.bn_buf[:, 0:6 * nbn]))
                    step(lambda: v.tensor_scalar(out=self.stats[:, 2:3], in0=self.stats[:, 1:2],
                                                 scalar1=1e-5, scalar2=None, op0=ALU.add))
                s_v.v = v_pre + 3

                @blk.scalar
                def _(s):
                    s.wait_ge(s_v.h, s_v.v)
                    s.activation(self.stats[:, 3:4], self.stats[:, 2:3], AF.Sqrt
                                 ).then_inc(s_a.h, 1)
                s_a.v += 1

                v_pre = s_v.v

                @blk.vector
                def _(v, v_pre=v_pre):
                    v.wait_ge(s_a.h, s_a.v)
                    cnt = v_pre

                    def step(op):
                        nonlocal cnt
                        op().then_inc(s_v.h, 1)
                        cnt += 1
                        v.wait_ge(s_v.h, cnt)
                    step(lambda: v.reciprocal(self.stats[:, 4:5], self.stats[:, 3:4]))
                    yc = 4
                    for it in range(2):
                        step(lambda yc=yc: v.tensor_tensor(
                            out=self.stats[:, 5:6], in0=self.stats[:, yc:yc + 1],
                            in1=self.stats[:, yc:yc + 1], op=ALU.mult))
                        step(lambda: v.tensor_tensor(
                            out=self.stats[:, 6:7], in0=self.stats[:, 5:6],
                            in1=self.stats[:, 2:3], op=ALU.mult))
                        step(lambda: v.tensor_scalar(
                            out=self.stats[:, 6:7], in0=self.stats[:, 6:7],
                            scalar1=-0.5, scalar2=1.5, op0=ALU.mult, op1=ALU.add))
                        step(lambda yc=yc, it=it: v.tensor_tensor(
                            out=self.stats[:, 7 + it:8 + it], in0=self.stats[:, yc:yc + 1],
                            in1=self.stats[:, 6:7], op=ALU.mult))
                        yc = 7 + it
                    step(lambda: v.tensor_tensor(out=self.stats[:, 9:10],
                                                 in0=self.stats[:, 0:1],
                                                 in1=self.stats[:, 8:9], op=ALU.mult))
                    step(lambda: v.tensor_scalar(out=self.stats[:, 10:11],
                                                 in0=self.stats[:, 9:10],
                                                 scalar1=-1.0, scalar2=None, op0=ALU.mult))
                s_v.v = v_pre + 11

                @blk.scalar
                def _(s):
                    s.wait_ge(s_v.h, s_v.v)
                    s.activation(self.zt_buf[:, :], self.xt_buf[:, :], AF.Identity,
                                 bias=self.stats[:, 10:11], scale=self.stats[:, 8:9]
                                 ).then_inc(s_a.h, 1)
                s_a.v += 1

                t_pre, v_pre = s_t.v, s_v.v

                @blk.tensor
                def _(t, k=k, t_pre=t_pre, v_pre=v_pre):
                    t.wait_ge(s_a.h, s_a.v)
                    for cj in range(c.CT):
                        pst = self.psT1 if cj % 2 == 0 else self.psT2
                        t.wait_ge(s_v.h, max(0, v_pre + 2 * (cj - 2) + 2 if cj >= 2 else v_pre - 1))
                        t.transpose(pst[:, 0:128], self.zt_buf[:, 128 * cj:128 * cj + 128],
                                    self.iden[:, :]).then_inc(s_t.h, 1)

                @blk.vector
                def _(v, k=k, t_pre=t_pre, v_pre=v_pre):
                    cnt2 = v_pre
                    for cj in range(c.CT):
                        pst = self.psT1 if cj % 2 == 0 else self.psT2
                        sl = slice(cj * c.TPC + 128 * k, cj * c.TPC + 128 * k + 128)
                        v.wait_ge(s_t.h, t_pre + cj + 1)
                        v.tensor_copy(self.z_hi[:, sl], pst[:, 0:128]).then_inc(s_v.h, 1)
                        cnt2 += 1
                        v.wait_ge(s_v.h, cnt2)
                        v.scalar_tensor_tensor(out=self.z_lo[:, sl], in0=self.z_hi[:, sl],
                                               scalar=-1.0, in1=pst[:, 0:128],
                                               op0=ALU.mult, op1=ALU.add).then_inc(s_v.h, 1)
                        cnt2 += 1
                s_t.v += c.CT
                s_v.v += 2 * c.CT

    def x_like(self, src, k, fj=None):
        c = self.c
        if fj is None:
            return ap2(src, 128 * k * c.D, (c.D, 128), (1, c.D))
        return ap2(src, 128 * k * c.D + 128 * fj, (c.D, 128), (1, 128))

    def w_load(self, sy, dram, cj, fj, rowstride=None):
        c = self.c
        rs = c.D if rowstride is None else rowstride
        i = self.wload
        b = i - self.NWS
        if b >= 0 and self.unit_rel[b] > 0:
            sy.wait_ge(self.s_wf.h, self.unit_rel[b])
        slot = i % self.NWS
        sy.dma_start(out=self.wslots[:, 128 * slot:128 * slot + 128],
                     in_=ap2(dram, 128 * cj * rs + 128 * fj, (rs, 128), (1, 128))
                     ).then_inc(self.s_wl[slot].h, 16)
        self.wl_cnt[slot] += 16
        self.unit_rel.append(i + 1)
        self.wload += 1
        return slot, self.wl_cnt[slot]

    def ffn_phase(self, src_x, res_x, dst):
        c, nc = self.c, self.nc
        s_ld, s_st, s_t, s_v = self.s_ld, self.s_st, self.s_t, self.s_v
        with nc.Block() as blk:
            self.z_build(blk, src_x)
            loads = []

            @blk.sync
            def _(sy):
                for cj in range(c.CT):
                    loads.append(self.w_load(sy, self.acat, cj, 0, rowstride=128))

            v_pre = s_v.v
            splits = [v_pre + 2 * (i + 1) for i in range(c.CT)]

            @blk.vector
            def _(v, v_pre=v_pre):
                cnt = v_pre
                for i in range(c.CT):
                    slot, lv = loads[i]
                    v.wait_ge(self.s_wl[slot].h, lv)
                    ws = self.wslots[:, 128 * slot:128 * slot + 128]
                    v.tensor_copy(self.w_hi[:, 128 * slot:128 * slot + 128], ws
                                  ).then_inc(s_v.h, 1)
                    cnt += 1
                    v.wait_ge(s_v.h, cnt)
                    v.scalar_tensor_tensor(out=self.w_lo[:, 128 * slot:128 * slot + 128],
                                           in0=self.w_hi[:, 128 * slot:128 * slot + 128],
                                           scalar=-1.0, in1=ws,
                                           op0=ALU.mult, op1=ALU.add).then_inc(s_v.h, 1)
                    cnt += 1
            s_v.v += 2 * c.CT
            z_gate0 = s_v.v

            @blk.tensor
            def _(t):
                for cj in range(c.CT):
                    slot, lv = loads[cj]
                    t.wait_ge(s_v.h, splits[cj])
                    whi = self.w_hi[:, 128 * slot:128 * slot + 128]
                    wlo = self.w_lo[:, 128 * slot:128 * slot + 128]
                    sls = [slice(cj * c.TPC + ch * 512, cj * c.TPC + ch * 512 + c.CHW)
                           for ch in range(c.NCH)]
                    pbs = [self.psB if ch == 0 else self.psB2 for ch in range(c.NCH)]
                    for ch in range(c.NCH):
                        t.matmul(pbs[ch][:, 0:c.CHW], whi, self.z_hi[:, sls[ch]],
                                 start=(cj == 0), stop=False)
                        t.matmul(pbs[ch][:, 0:c.CHW], whi, self.z_lo[:, sls[ch]],
                                 start=False, stop=False)
                    for ch in range(c.NCH):
                        inst = t.matmul(pbs[ch][:, 0:c.CHW], wlo, self.z_hi[:, sls[ch]],
                                        start=False, stop=(cj == c.CT - 1))
                    inst.then_inc(self.s_wf.h, 1)
            wf_done = self.wload

            @blk.vector
            def _(v, wf_done=wf_done):
                v.wait_ge(self.s_wf.h, wf_done)
                cnt = s_v.v
                for ch in range(c.NCH):
                    psb = self.psB if ch == 0 else self.psB2
                    tb = self.ffn_buf[:, ch * 512:ch * 512 + c.CHW]
                    hh = self.hsc_hi[:, ch * 512:ch * 512 + c.CHW]
                    hl = self.hsc_lo[:, ch * 512:ch * 512 + c.CHW]
                    v.tensor_scalar(out=tb, in0=psb[:, 0:c.CHW],
                                    scalar1=self.coefs[:, 0:1], scalar2=None,
                                    op0=ALU.mult).then_inc(s_v.h, 1)
                    cnt += 1
                    v.wait_ge(s_v.h, cnt)
                    v.tensor_copy(hh, tb).then_inc(s_v.h, 1)
                    cnt += 1
                    v.wait_ge(s_v.h, cnt)
                    v.scalar_tensor_tensor(out=hl, in0=hh, scalar=-1.0, in1=tb,
                                           op0=ALU.mult, op1=ALU.add).then_inc(s_v.h, 1)
                    cnt += 1
            s_v.v += 3 * c.NCH
            self.proj_pass(blk, self.wf, dst, res_x, half_bias=True, lora=True)

    def proj_mm(self, blk, wdram, fj, half_bias, lora, pst8):
        c = self.c
        s_ld, s_t, s_v = self.s_ld, self.s_t, self.s_v
        psA = self.psA if fj % 2 == 0 else self.psB
        psA2 = self.psA2 if fj % 2 == 0 else self.psB2
        loads = []

        @blk.sync
        def _(sy, fj=fj):
            for cj in range(c.CT):
                loads.append(self.w_load(sy, wdram, cj, fj))
            if lora:
                loads.append(self.w_load(sy, self.bcat, 0, fj, rowstride=c.D))

        nld = len(loads) if loads else c.CT + (1 if lora else 0)
        v_pre = s_v.v
        splits = [v_pre + 2 * (i + 1) for i in range(c.CT + (1 if lora else 0))]

        @blk.vector
        def _(v, fj=fj, v_pre=v_pre):
            cnt = v_pre
            for i in range(c.CT + (1 if lora else 0)):
                slot, lv = loads[i]
                v.wait_ge(self.s_wl[slot].h, lv)
                ws = self.wslots[:, 128 * slot:128 * slot + 128]
                v.tensor_copy(self.w_hi[:, 128 * slot:128 * slot + 128], ws
                              ).then_inc(s_v.h, 1)
                cnt += 1
                v.wait_ge(s_v.h, cnt)
                v.scalar_tensor_tensor(out=self.w_lo[:, 128 * slot:128 * slot + 128],
                                       in0=self.w_hi[:, 128 * slot:128 * slot + 128],
                                       scalar=-1.0, in1=ws,
                                       op0=ALU.mult, op1=ALU.add).then_inc(s_v.h, 1)
                cnt += 1
        s_v.v += 2 * (c.CT + (1 if lora else 0))

        mm_gate = max(pst8["z_gate"],
                      pst8["evac_inc"][fj - 2] if fj >= 2 else 0)

        @blk.tensor
        def _(t, fj=fj, mm_gate=mm_gate):
            t.wait_ge(s_v.h, mm_gate)
            first = True
            for cj in range(c.CT):
                slot, lv = loads[cj]
                t.wait_ge(s_v.h, splits[cj])
                whi = self.w_hi[:, 128 * slot:128 * slot + 128]
                wlo = self.w_lo[:, 128 * slot:128 * slot + 128]
                last = (cj == c.CT - 1) and not lora
                sls = [slice(cj * c.TPC + ch * 512, cj * c.TPC + ch * 512 + c.CHW)
                       for ch in range(c.NCH)]
                pas = [psA if ch == 0 else psA2 for ch in range(c.NCH)]
                for ch in range(c.NCH):
                    t.matmul(pas[ch][:, 0:c.CHW], whi, self.z_hi[:, sls[ch]],
                             start=(cj == 0), stop=False)
                    t.matmul(pas[ch][:, 0:c.CHW], whi, self.z_lo[:, sls[ch]],
                             start=False, stop=False)
                for ch in range(c.NCH):
                    inst = t.matmul(pas[ch][:, 0:c.CHW], wlo, self.z_hi[:, sls[ch]],
                                    start=False, stop=last)
                inst.then_inc(self.s_wf.h, 1)
            if lora:
                slot, lv = loads[c.CT]
                t.wait_ge(s_v.h, splits[c.CT])
                whi = self.w_hi[:, 128 * slot:128 * slot + 128]
                wlo = self.w_lo[:, 128 * slot:128 * slot + 128]
                sls = [slice(ch * 512, ch * 512 + c.CHW) for ch in range(c.NCH)]
                pas = [psA if ch == 0 else psA2 for ch in range(c.NCH)]
                for ch in range(c.NCH):
                    t.matmul(pas[ch][:, 0:c.CHW], whi, self.hsc_hi[:, sls[ch]],
                             start=False, stop=False)
                    t.matmul(pas[ch][:, 0:c.CHW], whi, self.hsc_lo[:, sls[ch]],
                             start=False, stop=False)
                for ch in range(c.NCH):
                    inst = t.matmul(pas[ch][:, 0:c.CHW], wlo, self.hsc_hi[:, sls[ch]],
                                    start=False, stop=True)
                inst.then_inc(self.s_wf.h, 1)
        wf_done = self.wload
        fb = max(self.c.TPC, 512) * (fj % 2)

        @blk.vector
        def _(v, fj=fj, wf_done=wf_done, fb=fb):
            v.wait_ge(self.s_wf.h, wf_done)
            if fj >= 2:
                v.wait_ge(s_t.h, pst8["transp_end"][fj - 2])
            for ch in range(c.NCH):
                pa = psA if ch == 0 else psA2
                if half_bias:
                    inst = v.tensor_scalar(out=self.ffn_buf[:, fb + ch * 512:fb + ch * 512 + c.CHW],
                                           in0=pa[:, 0:c.CHW], scalar1=0.5,
                                           scalar2=self.hbias[:, fj:fj + 1],
                                           op0=ALU.mult, op1=ALU.add)
                else:
                    inst = v.tensor_copy(self.ffn_buf[:, fb + ch * 512:fb + ch * 512 + c.CHW],
                                         pa[:, 0:c.CHW])
            inst.then_inc(s_v.h, 1)
        s_v.v += 1
        pst8["evac_inc"].append(s_v.v)

    def proj_ep(self, blk, fj, dst, res_x, pst8):
        c = self.c
        s_ld, s_st, s_t, s_v = self.s_ld, self.s_st, self.s_t, self.s_v
        fb = max(self.c.TPC, 512) * (fj % 2)
        ep_gate = pst8["add_end"][fj - 1] if fj >= 1 else 0

        @blk.sync
        def _(sy, fj=fj, ep_gate=ep_gate):
            if ep_gate > 0:
                sy.wait_ge(s_v.h, ep_gate)
            for k in range(c.NSLOT):
                sy.dma_start(out=self.ep_x[:, 128 * k:128 * k + 128],
                             in_=self.x_like(res_x, k, fj)).then_inc(self.s_ep.h, 16)
                self.s_ep.v += 16
        epl_end = self.s_ep.v
        t_pre, v_pre = s_t.v, s_v.v
        ev = pst8["evac_inc"][fj]

        @blk.tensor
        def _(t, fj=fj, t_pre=t_pre, v_pre=v_pre, ev=ev):
            t.wait_ge(s_v.h, ev)
            for k in range(c.NSLOT):
                pst = self.psT1 if k % 2 == 0 else self.psT2
                t.wait_ge(s_v.h, max(0, v_pre + k - 1))
                t.transpose(pst[:, 0:128], self.ffn_buf[:, fb + 128 * k:fb + 128 * k + 128],
                            self.iden[:, :]).then_inc(s_t.h, 1)
        s_t.v += c.NSLOT
        pst8["transp_end"].append(s_t.v)
        st_pre = s_st.v

        @blk.vector
        def _(v, fj=fj, t_pre=t_pre, st_pre=st_pre, epl_end=epl_end):
            v.wait_ge(self.s_ep.h, epl_end)
            v.wait_ge(s_st.h, st_pre)
            for k in range(c.NSLOT):
                pst = self.psT1 if k % 2 == 0 else self.psT2
                v.wait_ge(s_t.h, t_pre + k + 1)
                v.tensor_add(self.ep_o[:, 128 * k:128 * k + 128],
                             pst[:, 0:128],
                             self.ep_x[:, 128 * k:128 * k + 128]
                             ).then_inc(s_v.h, 1)
        base_v = s_v.v
        s_v.v += c.NSLOT
        pst8["add_end"].append(s_v.v)

        @blk.scalar
        def _(s, fj=fj, base_v=base_v):
            for k in range(c.NSLOT):
                s.wait_ge(s_v.h, base_v + k + 1)
                s.dma_start(out=self.x_like(dst, k, fj),
                            in_=self.ep_o[:, 128 * k:128 * k + 128]
                            ).then_inc(s_st.h, 16)
        s_st.v += 16 * c.NSLOT

    def proj_pass(self, blk, wdram, dst, res_x, half_bias, lora):
        c = self.c
        pst8 = {"z_gate": self.s_v.v, "evac_inc": [], "transp_end": [], "add_end": []}
        for fj in range(c.CT):
            self.proj_mm(blk, wdram, fj, half_bias, lora, pst8)
            if fj >= 1:
                self.proj_ep(blk, fj - 1, dst, res_x, pst8)
        self.proj_ep(blk, c.CT - 1, dst, res_x, pst8)

    def p2_qkv(self):
        c, nc = self.c, self.nc
        s_ld, s_st, s_gd, s_t, s_v, s_a, s_g = (self.s_ld, self.s_st, self.s_gd,
                                                self.s_t, self.s_v, self.s_a, self.s_g)
        with nc.Block() as blk:
            self.z_build(blk, self.x1_dr)
            ld0 = s_ld.v

            @blk.sync
            def _(sy):
                sy.dma_start(out=self.sc1[:, 0:c.NSLOT],
                             in_=ap2(self.tstart_d, 0, (c.NSLOT, 1), (1, c.NSLOT))
                             ).then_inc(s_ld.h, 16)
                sy.dma_start(out=self.qrow[0:1, 0:c.TPC],
                             in_=ap2(self.tmask_d, 0, (c.TPC, 1), (1, c.TPC))
                             ).then_inc(s_ld.h, 16)
            s_ld.v += 32

            g_pre = s_g.v

            @blk.gpsimd
            def _(g, ld0=ld0):
                g.wait_ge(s_ld.h, ld0 + 32)
                for s in range(c.NSLOT):
                    g.iota(self.krow[0:1, 128 * s:128 * s + 128], [[1, 128]], base=0,
                           channel_multiplier=0, allow_small_or_imprecise_dtypes=True)
                g.iota(self.stats[0:64, 16:17], [[0, 1]], base=0, channel_multiplier=1,
                       allow_small_or_imprecise_dtypes=True)
                g.iota(self.stats[64:128, 16:17], [[0, 1]], base=0, channel_multiplier=1,
                       allow_small_or_imprecise_dtypes=True).then_inc(s_g.h, 1)
            s_g.v += 1

            v_pre = s_v.v

            @blk.vector
            def _(v, v_pre=v_pre):
                v.wait_ge(s_g.h, s_g.v)
                cnt = v_pre

                def step(op):
                    nonlocal cnt
                    op().then_inc(s_v.h, 1)
                    cnt += 1
                    v.wait_ge(s_v.h, cnt)
                for s in range(c.NSLOT):
                    inst = v.tensor_scalar(out=self.krow[0:1, 128 * s:128 * s + 128],
                                           in0=self.krow[0:1, 128 * s:128 * s + 128],
                                           scalar1=self.sc1[0:1, s:s + 1], scalar2=None,
                                           op0=ALU.add)
                inst.then_inc(s_v.h, 1)
                cnt += 1
                v.wait_ge(s_v.h, cnt)
                step(lambda: v.tensor_scalar(out=self.stats[:, 17:18],
                                             in0=self.stats[:, 16:17],
                                             scalar1=-math.log(c.ROPE_THETA) / 64.0,
                                             scalar2=None, op0=ALU.mult))
            s_v.v = v_pre + 2

            t_pre = s_t.v

            @blk.tensor
            def _(t, t_pre=t_pre):
                t.wait_ge(s_v.h, s_v.v)
                n = 0
                for src, n0 in ((self.krow, 0), (self.qrow, c.NCH)):
                    for ch in range(c.NCH):
                        pa = self.psT1 if (n0 + ch) % 2 == 0 else self.psT2
                        t.wait_ge(s_v.h, max(0, s_v.v + n0 + ch - 1))
                        t.matmul(pa[:, 0:c.CHW], self.ones[0:1, :],
                                 src[0:1, ch * 512:ch * 512 + c.CHW],
                                 start=True, stop=True).then_inc(s_t.h, 1)

            @blk.vector
            def _(v, t_pre=t_pre):
                for dst, n0 in ((self.rope_a, 0), (self.tmask_b, c.NCH)):
                    for ch in range(c.NCH):
                        pa = self.psT1 if (n0 + ch) % 2 == 0 else self.psT2
                        v.wait_ge(s_t.h, t_pre + n0 + ch + 1)
                        v.tensor_copy(dst[:, ch * 512:ch * 512 + c.CHW], pa[:, 0:c.CHW]
                                      ).then_inc(s_v.h, 1)
            s_t.v += 2 * c.NCH
            s_v.v += 2 * c.NCH

            @blk.scalar
            def _(s):
                s.wait_ge(s_v.h, s_v.v)
                s.activation(self.stats[:, 18:19], self.stats[:, 17:18], AF.Exp
                             ).then_inc(s_a.h, 1)
            s_a.v += 1

            v_pre = s_v.v

            @blk.vector
            def _(v, v_pre=v_pre):
                v.wait_ge(s_a.h, s_a.v)
                cnt = v_pre

                def step(op):
                    nonlocal cnt
                    op().then_inc(s_v.h, 1)
                    cnt += 1
                    v.wait_ge(s_v.h, cnt)
                step(lambda: v.tensor_scalar(out=self.rope_b[:, :], in0=self.rope_a[:, :],
                                             scalar1=self.stats[:, 18:19], scalar2=None,
                                             op0=ALU.mult))
                step(lambda: v.tensor_scalar(out=self.stats[:, 19:20], in0=self.piota[:, :],
                                             scalar1=63.5, scalar2=2.0,
                                             op0=ALU.is_gt, op1=ALU.mult))
                step(lambda: v.tensor_scalar(out=self.stats[:, 19:20],
                                             in0=self.stats[:, 19:20],
                                             scalar1=-1.0, scalar2=None, op0=ALU.add))
            s_v.v = v_pre + 3

            for which in ("sin", "cos"):
                v_pre = s_v.v

                @blk.vector
                def _(v, which=which, v_pre=v_pre):
                    v.wait_ge(s_t.h, s_t.v)
                    v.wait_ge(s_a.h, s_a.v)
                    cnt = v_pre

                    def step(op):
                        nonlocal cnt
                        op().then_inc(s_v.h, 1)
                        cnt += 1
                        v.wait_ge(s_v.h, cnt)
                    add = 0.25 if which == "cos" else 0.0
                    step(lambda: v.tensor_scalar(out=self.rope_a[:, :], in0=self.rope_b[:, :],
                                                 scalar1=1.0 / (2.0 * math.pi), scalar2=add,
                                                 op0=ALU.mult, op1=ALU.add))
                    p = 1 << 11
                    while p >= 1:
                        step(lambda p=p: v.tensor_scalar(
                            out=self.zt_buf[:, 0:c.TPC], in0=self.rope_a[:, :],
                            scalar1=float(p), scalar2=None, op0=ALU.is_ge))
                        step(lambda p=p: v.scalar_tensor_tensor(
                            out=self.rope_a[:, :], in0=self.zt_buf[:, 0:c.TPC],
                            scalar=-float(p), in1=self.rope_a[:, :],
                            op0=ALU.mult, op1=ALU.add))
                        p //= 2
                    step(lambda: v.tensor_scalar(out=self.zt_buf[:, 0:c.TPC],
                                                 in0=self.rope_a[:, :],
                                                 scalar1=0.5, scalar2=None, op0=ALU.is_ge))
                    step(lambda: v.tensor_tensor(out=self.rope_a[:, :], in0=self.rope_a[:, :],
                                                 in1=self.zt_buf[:, 0:c.TPC],
                                                 op=ALU.subtract))
                    step(lambda: v.tensor_scalar(out=self.rope_a[:, :], in0=self.rope_a[:, :],
                                                 scalar1=2.0 * math.pi, scalar2=None,
                                                 op0=ALU.mult))
                s_v.v = v_pre + 28

                @blk.scalar
                def _(s, which=which):
                    s.wait_ge(s_v.h, s_v.v)
                    dst = self.sin_t if which == "sin" else self.cos_t
                    s.activation(dst[:, :], self.rope_a[:, :], AF.Sin
                                 ).then_inc(s_a.h, 1)
                s_a.v += 1

            @blk.vector
            def _(v):
                v.wait_ge(s_a.h, s_a.v)
                v.tensor_scalar(out=self.sin_t[:, :], in0=self.sin_t[:, :],
                                scalar1=self.stats[:, 19:20], scalar2=None, op0=ALU.mult
                                ).then_inc(s_v.h, 1)
            s_v.v += 1

            self.head_evac = []
            self.head_gate0 = s_v.v
            for h in range(c.H):
                self.proj_head(blk, self.wk, h, is_q=False)
            self.v_proj(blk)
            kv_st = s_st.v

            @blk.gpsimd
            def _(g, kv_st=kv_st):
                g.wait_ge(s_st.h, kv_st)
                g.collective_compute(
                    "AllGather", ALU.bypass, replica_groups=[list(range(c.NCORE))],
                    ins=[self.kv_in[:].opt()], outs=[self.kv_out[:].opt()],
                ).then_inc(self.s_cc.h, 1)
            self.s_cc.v += 1
            self.head_evac = []
            self.head_gate0 = s_v.v
            for h in range(c.H):
                self.proj_head(blk, self.wq, h, is_q=True)
            qv_done = s_v.v

            @blk.scalar
            def _(s, qv_done=qv_done):
                s.wait_ge(s_v.h, qv_done)
                s.dma_start(out=ap2(self.qb_in, 0, (c.H, 128), (1, c.H)),
                            in_=self.qbar[:, :]).then_inc(s_st.h, 16)
            s_st.v += 16
            qb_st = s_st.v

            @blk.gpsimd
            def _(g, qb_st=qb_st):
                g.wait_ge(s_st.h, qb_st)
                g.collective_compute(
                    "AllGather", ALU.bypass, replica_groups=[list(range(c.NCORE))],
                    ins=[self.qb_in[:].opt()], outs=[self.qb_out[:].opt()],
                ).then_inc(self.s_cc.h, 1)
            self.s_cc.v += 1

    def proj_head(self, blk, wdram, h, is_q):
        c, nc = self.c, self.nc
        s_ld, s_st, s_gd, s_t, s_v = self.s_ld, self.s_st, self.s_gd, self.s_t, self.s_v
        row = self.qrow if is_q else self.krow
        if True:
            loads = []

            @blk.sync
            def _(sy, h=h):
                for cj in range(c.CT):
                    loads.append(self.w_load(sy, wdram, cj, h))

            v_pre = s_v.v
            splits = [v_pre + 2 * (i + 1) for i in range(c.CT)]

            @blk.vector
            def _(v, h=h, v_pre=v_pre):
                cnt = v_pre
                for i in range(c.CT):
                    slot, lv = loads[i]
                    v.wait_ge(self.s_wl[slot].h, lv)
                    ws = self.wslots[:, 128 * slot:128 * slot + 128]
                    v.tensor_copy(self.w_hi[:, 128 * slot:128 * slot + 128], ws
                                  ).then_inc(s_v.h, 1)
                    cnt += 1
                    v.wait_ge(s_v.h, cnt)
                    v.scalar_tensor_tensor(out=self.w_lo[:, 128 * slot:128 * slot + 128],
                                           in0=self.w_hi[:, 128 * slot:128 * slot + 128],
                                           scalar=-1.0, in1=ws,
                                           op0=ALU.mult, op1=ALU.add).then_inc(s_v.h, 1)
                    cnt += 1
            s_v.v += 2 * c.CT
            mm_gate = self.head_evac[h - 2] if h >= 2 else self.head_gate0

            @blk.tensor
            def _(t, h=h, mm_gate=mm_gate):
                t.wait_ge(s_v.h, mm_gate)
                psA = self.psA if h % 2 == 0 else self.psB
                psA2 = self.psA2 if h % 2 == 0 else self.psB2
                for cj in range(c.CT):
                    slot, lv = loads[cj]
                    t.wait_ge(s_v.h, splits[cj])
                    whi = self.w_hi[:, 128 * slot:128 * slot + 128]
                    wlo = self.w_lo[:, 128 * slot:128 * slot + 128]
                    sls = [slice(cj * c.TPC + ch * 512, cj * c.TPC + ch * 512 + c.CHW)
                           for ch in range(c.NCH)]
                    pas = [psA if ch == 0 else psA2 for ch in range(c.NCH)]
                    for ch in range(c.NCH):
                        t.matmul(pas[ch][:, 0:c.CHW], whi, self.z_hi[:, sls[ch]],
                                 start=(cj == 0), stop=False)
                        t.matmul(pas[ch][:, 0:c.CHW], whi, self.z_lo[:, sls[ch]],
                                 start=False, stop=False)
                    for ch in range(c.NCH):
                        inst = t.matmul(pas[ch][:, 0:c.CHW], wlo, self.z_hi[:, sls[ch]],
                                        start=False, stop=(cj == c.CT - 1))
                    inst.then_inc(self.s_wf.h, 1)
            wf_done = self.wload

            @blk.vector
            def _(v, h=h, wf_done=wf_done):
                v.wait_ge(self.s_wf.h, wf_done)
                v.wait_ge(s_gd.h, s_gd.v)
                v.wait_ge(s_st.h, s_st.v)
                psA = self.psA if h % 2 == 0 else self.psB
                psA2 = self.psA2 if h % 2 == 0 else self.psB2
                for ch in range(c.NCH):
                    pa = psA if ch == 0 else psA2
                    inst = v.tensor_copy(row[:, ch * 512:ch * 512 + c.CHW], pa[:, 0:c.CHW])
                inst.then_inc(s_v.h, 1)
            s_v.v += 1
            self.head_evac.append(s_v.v)

            @blk.gpsimd
            def _(g):
                g.wait_ge(s_v.h, s_v.v)
                g.dma_start(out=self.rope_a[0:64, :], in_=row[64:128, :]).then_inc(s_gd.h, 16)
                g.dma_start(out=self.rope_a[64:128, :], in_=row[0:64, :]).then_inc(s_gd.h, 16)
            s_gd.v += 32

            v_pre = s_v.v

            @blk.vector
            def _(v, h=h, v_pre=v_pre):
                v.wait_ge(s_gd.h, s_gd.v)
                v.tensor_tensor(out=self.rope_b[:, :], in0=row[:, :],
                                in1=self.cos_t[:, :], op=ALU.mult)
                v.tensor_tensor(out=self.rope_a[:, :], in0=self.rope_a[:, :],
                                in1=self.sin_t[:, :], op=ALU.mult).then_inc(s_v.h, 1)
                v.wait_ge(s_v.h, v_pre + 1)
                v.tensor_add(row[:, :], self.rope_b[:, :], self.rope_a[:, :]
                             ).then_inc(s_v.h, 1)
                if is_q:
                    v.wait_ge(s_v.h, v_pre + 2)
                    v.tensor_tensor(out=self.rope_b[:, :], in0=row[:, :],
                                    in1=self.tmask_b[:, :], op=ALU.mult
                                    ).then_inc(s_v.h, 1)
                    v.wait_ge(s_v.h, v_pre + 3)
                    v.reduce_sum(self.qbar[:, h:h + 1], self.rope_b[:, :],
                                 axis=AX.X).then_inc(s_v.h, 1)
            s_v.v += 4 if is_q else 2

            @blk.scalar
            def _(s, h=h):
                s.wait_ge(s_v.h, s_v.v)
                if is_q:
                    s.dma_start(out=ap2(self.q_dr, 128 * h * c.TPC, (c.TPC, 128), (1, c.TPC)),
                                in_=row[:, :]).then_inc(s_st.h, 16)
                else:
                    s.dma_start(out=ap2(self.kv_in, 128 * h * c.TPC, (c.TPC, 128), (1, c.TPC)),
                                in_=row[:, :]).then_inc(s_st.h, 16)
            s_st.v += 16

    def v_proj(self, blk):
        c, nc = self.c, self.nc
        s_ld, s_st, s_t, s_v = self.s_ld, self.s_st, self.s_t, self.s_v
        pss = [self.psA, self.psA2, self.psB, self.psB2]
        t_base, v_base, st_base = s_t.v, s_v.v, s_st.v
        vevac = []
        grp = 0
        for k in range(c.NSLOT):
            for fc in range(c.NFC):
                if True:
                    loads = []
                    mygrp = grp
                    grp += 1

                    @blk.sync
                    def _(sy, k=k, fc=fc):
                        for cj in range(c.CT):
                            i = self.wload
                            assert i % 4 == 0
                            b = i - self.NWS + 3
                            if b >= 0 and self.unit_rel[b] > 0:
                                sy.wait_ge(self.s_wf.h, self.unit_rel[b])
                            slot = i % 8
                            sy.dma_start(
                                out=self.wslots[:, 128 * slot:128 * slot + 512],
                                in_=ap2(self.wv, 128 * cj * c.D + 512 * fc, (c.D, 128), (1, 512))
                                ).then_inc(self.s_wl[slot].h, 16)
                            self.wl_cnt[slot] += 16
                            loads.append((slot, self.wl_cnt[slot]))
                            self.unit_rel.extend([i + 4] * 4)
                            self.wload += 4

                    wf_done = [0]
                    v_pre2 = s_v.v
                    splits = [v_pre2 + 2 * (i + 1) for i in range(c.CT)]

                    @blk.vector
                    def _(v, k=k, fc=fc, v_pre2=v_pre2):
                        cnt = v_pre2
                        for i in range(c.CT):
                            slot, lv = loads[i]
                            v.wait_ge(self.s_wl[slot].h, lv)
                            ws = self.wslots[:, 128 * slot:128 * slot + 512]
                            v.tensor_copy(self.w_hi[:, 128 * slot:128 * slot + 512], ws
                                          ).then_inc(s_v.h, 1)
                            cnt += 1
                            v.wait_ge(s_v.h, cnt)
                            v.scalar_tensor_tensor(
                                out=self.w_lo[:, 128 * slot:128 * slot + 512],
                                in0=self.w_hi[:, 128 * slot:128 * slot + 512],
                                scalar=-1.0, in1=ws,
                                op0=ALU.mult, op1=ALU.add).then_inc(s_v.h, 1)
                            cnt += 1
                    s_v.v += 2 * c.CT

                    @blk.tensor
                    def _(t, k=k, fc=fc, mygrp=mygrp):
                        pa = pss[mygrp % 4]
                        if mygrp >= 4:
                            t.wait_ge(s_v.h, vevac[mygrp - 4])
                        for cj in range(c.CT):
                            slot, lv = loads[cj]
                            t.wait_ge(s_v.h, splits[cj])
                            zhi = self.z_hi[:, cj * c.TPC + 128 * k:cj * c.TPC + 128 * k + 128]
                            zlo = self.z_lo[:, cj * c.TPC + 128 * k:cj * c.TPC + 128 * k + 128]
                            whi = self.w_hi[:, 128 * slot:128 * slot + 512]
                            wlo = self.w_lo[:, 128 * slot:128 * slot + 512]
                            t.matmul(pa[:, 0:512], zhi, whi, start=(cj == 0), stop=False)
                            t.matmul(pa[:, 0:512], zhi, wlo, start=False, stop=False)
                            t.matmul(pa[:, 0:512], zlo, whi,
                                     start=False, stop=(cj == c.CT - 1)
                                     ).then_inc(self.s_wf.h, 4)
                        wf_done[0] = self.wload

                    @blk.vector
                    def _(v, k=k, fc=fc, mygrp=mygrp):
                        pa = pss[mygrp % 4]
                        v.wait_ge(self.s_wf.h, wf_done[0])
                        if mygrp >= 2:
                            v.wait_ge(s_st.h, st_base + 16 * (mygrp - 1))
                        v.tensor_copy(self.ffn_buf[:, 512 * (mygrp % 2):512 * (mygrp % 2) + 512],
                                      pa[:, 0:512]).then_inc(s_v.h, 1)
                    s_v.v += 1
                    vevac.append(s_v.v)

                    @blk.scalar
                    def _(s, k=k, fc=fc, mygrp=mygrp):
                        s.wait_ge(s_v.h, vevac[mygrp])
                        s.dma_start(
                            out=ap2(self.kv_in, c.VOFF + 128 * k * c.D + 512 * fc,
                                    (c.D, 128), (1, 512)),
                            in_=self.ffn_buf[:, 512 * (mygrp % 2):512 * (mygrp % 2) + 512]
                            ).then_inc(s_st.h, 16)
        n = c.NSLOT * c.NFC
        s_st.v += 16 * n

    def p3_topk(self):
        c, nc = self.c, self.nc
        s_ld, s_t, s_v, s_g, s_gd = self.s_ld, self.s_t, self.s_v, self.s_g, self.s_gd
        with nc.Block() as blk:
            ld0 = s_ld.v

            @blk.sync
            def _(sy, ld0=ld0):
                sy.wait_ge(self.s_cc.h, 2)
                sy.dma_start(out=self.qb_all[:, :],
                             in_=ap3(self.qb_out, 0, (c.H, 128), (128 * c.H, c.NCORE), (1, c.H))
                             ).then_inc(s_ld.h, 16)
            s_ld.v += 16

            v_pre = s_v.v

            @blk.vector
            def _(v, ld0=ld0, v_pre=v_pre):
                v.wait_ge(s_ld.h, ld0 + 16)
                cnt = v_pre
                v.tensor_copy(self.qbar[:, :], self.qb_all[:, 0:c.H]).then_inc(s_v.h, 1)
                cnt += 1
                for r in range(1, c.NCORE):
                    v.wait_ge(s_v.h, cnt)
                    v.tensor_add(self.qbar[:, :], self.qbar[:, :],
                                 self.qb_all[:, r * c.H:(r + 1) * c.H]).then_inc(s_v.h, 1)
                    cnt += 1
            s_v.v += c.NCORE

            t_base = self.s_t.v
            for h in range(c.H):
                @blk.sync
                def _(sy, h=h):
                    if h > 0:
                        sy.wait_ge(self.s_t.h, t_base + h)
                    sy.dma_start(
                        out=ap3(self.kvis, 0, ((c.NVC + 1) * 128, 128), (128, c.NVC + 1), (1, 128)),
                        in_=ap3(self.kv_out, 128 * h * c.TPC,
                                (c.TPC, 128), (c.KVN, c.NVC + 1), (1, 128))
                        ).then_inc(self.s_kvi.h, 16)
                    self.s_kvi.v += 16

                @blk.tensor
                def _(t, h=h):
                    t.wait_ge(self.s_kvi.h, self.s_kvi.v)
                    t.wait_ge(s_v.h, s_v.v)
                    t.matmul(self.psM[0:1, 0:c.V0], self.qbar[:, h:h + 1],
                             self.kvis[:, c.VIS_START:c.VIS_START + c.V0],
                             start=(h == 0), stop=(h == c.H - 1)).then_inc(self.s_t.h, 1)
            self.s_t.v += c.H

            @blk.vector
            def _(v):
                v.wait_ge(s_t.h, s_t.v)
                v.tensor_copy(self.sc1[0:1, 0:c.V0], self.psM[0:1, 0:c.V0]
                              ).then_inc(s_v.h, 1)
            s_v.v += 1

            t_pre = s_t.v

            @blk.tensor
            def _(t, t_pre=t_pre):
                t.wait_ge(s_v.h, s_v.v)
                t.matmul(self.psT1[:, 0:c.V0], self.ones[0:1, :], self.sc1[0:1, 0:c.V0],
                         start=True, stop=True).then_inc(s_t.h, 1)
                for cc in range(c.NVC):
                    t.transpose(self.psT2[:, cc:cc + 1],
                                self.sc1[0:1, 128 * cc:128 * cc + 128],
                                self.iden[0:1, 0:1]).then_inc(s_t.h, 1)
            s_t.v += 1 + c.NVC

            g_pre = s_g.v

            @blk.gpsimd
            def _(g, g_pre=g_pre):
                g.iota(self.uio[:, :], [[1, c.V0]], base=0, channel_multiplier=0,
                       allow_small_or_imprecise_dtypes=True)
                for cc in range(c.NVC):
                    g.iota(self.vcol[:, cc:cc + 1], [[0, 1]], base=128 * cc,
                           channel_multiplier=1, allow_small_or_imprecise_dtypes=True)
                g.iota(self.colv[0:1, :], [[1, c.S]], base=0, channel_multiplier=0,
                       allow_small_or_imprecise_dtypes=True).then_inc(s_g.h, 1)
            s_g.v += 1

            v_pre = s_v.v

            @blk.vector
            def _(v, v_pre=v_pre):
                v.wait_ge(s_t.h, s_t.v)
                v.wait_ge(s_g.h, s_g.v)
                cnt = v_pre

                def step(op):
                    nonlocal cnt
                    op().then_inc(s_v.h, 1)
                    cnt += 1
                    v.wait_ge(s_v.h, cnt)
                step(lambda: v.tensor_copy(self.impb[:, :], self.psT1[:, 0:c.V0]))
                step(lambda: v.tensor_copy(self.impc[:, :], self.psT2[:, 0:c.NVC]))
                for cc in range(c.NVC):
                    step(lambda cc=cc: v.tensor_scalar(
                        out=self.g2_scr[:, :], in0=self.impb[:, :],
                        scalar1=self.impc[:, cc:cc + 1], scalar2=None,
                        op0=ALU.is_gt, accum_out=self.rankb[:, cc:cc + 1], op1=ALU.add))
                    step(lambda cc=cc: v.tensor_scalar(
                        out=self.g_scr[:, :], in0=self.impb[:, :],
                        scalar1=self.impc[:, cc:cc + 1], scalar2=None, op0=ALU.is_equal))
                    step(lambda cc=cc: v.scalar_tensor_tensor(
                        out=self.g2_scr[:, :], in0=self.uio[:, :],
                        scalar=self.vcol[:, cc:cc + 1], in1=self.g_scr[:, :],
                        op0=ALU.is_lt, op1=ALU.mult, accum_out=self.tieb[:, cc:cc + 1]))
                    step(lambda cc=cc: v.tensor_add(
                        self.rankb[:, cc:cc + 1], self.rankb[:, cc:cc + 1],
                        self.tieb[:, cc:cc + 1]))
                    step(lambda cc=cc: v.tensor_scalar(
                        out=self.keepc[:, cc:cc + 1], in0=self.rankb[:, cc:cc + 1],
                        scalar1=float(c.keep_now), scalar2=None, op0=ALU.is_lt))
            s_v.v = v_pre + 2 + 5 * c.NVC

            @blk.tensor
            def _(t):
                t.wait_ge(s_v.h, s_v.v)
                t.transpose(self.psT2[0:c.NVC, 0:128], self.keepc[:, 0:c.NVC],
                            self.iden[:, :]).then_inc(s_t.h, 1)
            s_t.v += 1

            @blk.vector
            def _(v):
                v.wait_ge(s_t.h, s_t.v)
                v.tensor_copy(self.keeprow[0:c.NVC, 0:128], self.psT2[0:c.NVC, 0:128]
                              ).then_inc(s_v.h, 1)
            s_v.v += 1

            @blk.gpsimd
            def _(g):
                g.wait_ge(s_v.h, s_v.v)
                for cc in range(c.NVC):
                    g.dma_start(out=self.sc3[0:1, 128 * cc:128 * cc + 128],
                                in_=self.keeprow[cc:cc + 1, 0:128]).then_inc(s_gd.h, 16)
            s_gd.v += 16 * c.NVC

            v_pre = s_v.v

            @blk.vector
            def _(v, v_pre=v_pre):
                v.wait_ge(s_gd.h, s_gd.v)
                v.wait_ge(s_g.h, s_g.v)
                v.tensor_scalar(out=self.sc3[0:1, 0:c.V0], in0=self.sc3[0:1, 0:c.V0],
                                scalar1=-1e9, scalar2=1e9, op0=ALU.mult, op1=ALU.add
                                ).then_inc(s_v.h, 1)
                v.wait_ge(s_v.h, v_pre + 1)
                v.tensor_add(self.colv[0:1, c.VIS_START:c.VIS_START + c.V0],
                             self.colv[0:1, c.VIS_START:c.VIS_START + c.V0],
                             self.sc3[0:1, 0:c.V0]).then_inc(s_v.h, 1)
            s_v.v += 2

            @blk.sync
            def _(sy):
                sy.wait_ge(s_v.h, s_v.v)
                sy.wait_ge(s_t.h, s_t.v)
                sy.dma_start(out=self.sc1[:, 0:c.NSLOT],
                             in_=ap2(self.tstart_d, 0, (c.NSLOT, 1), (1, c.NSLOT))
                             ).then_inc(s_ld.h, 16)
            s_ld.v += 16

            mspecs = [(kk, kk, None) for kk in range(min(2, c.NJC))]
            for s in range(c.NSLOT):
                for d in range(2):
                    kk = 2 * s + d
                    if kk < c.NJC:
                        mspecs.append((2 + 2 * s + d, kk, s))
            t_pre = s_t.v
            v_pre = s_v.v

            @blk.tensor
            def _(t, t_pre=t_pre, v_pre=v_pre):
                t.wait_ge(s_v.h, v_pre)
                t.wait_ge(s_ld.h, s_ld.v)
                t.matmul(self.psT2[:, 0:c.NSLOT], self.ones[0:1, :],
                         self.sc1[0:1, 0:c.NSLOT], start=True, stop=True).then_inc(s_t.h, 1)
                for n, (idx, kk, s) in enumerate(mspecs):
                    pa = self.psT1 if n % 2 == 0 else self.psM
                    t.wait_ge(s_v.h, max(0, v_pre + 1 + c.NSLOT + n - 1))
                    t.matmul(pa[:, 0:512], self.ones[0:1, :],
                             self.colv[0:1, 512 * kk:512 * kk + 512],
                             start=True, stop=True).then_inc(s_t.h, 1)

            @blk.vector
            def _(v, t_pre=t_pre, v_pre=v_pre):
                cnt = v_pre
                v.wait_ge(s_t.h, t_pre + 1)
                v.tensor_copy(self.tst_b[:, :], self.psT2[:, 0:c.NSLOT]).then_inc(s_v.h, 1)
                cnt += 1
                for s in range(c.NSLOT):
                    v.wait_ge(s_v.h, cnt)
                    v.tensor_scalar(out=self.iabs[:, s:s + 1], in0=self.piota[:, :],
                                    scalar1=self.tst_b[:, s:s + 1], scalar2=None,
                                    op0=ALU.add).then_inc(s_v.h, 1)
                    cnt += 1
                for n, (idx, kk, s) in enumerate(mspecs):
                    pa = self.psT1 if n % 2 == 0 else self.psM
                    v.wait_ge(s_t.h, t_pre + 1 + n + 1)
                    if s is None:
                        v.tensor_scalar(
                            out=self.maskt[:, 512 * idx:512 * idx + 512],
                            in0=pa[:, 0:512],
                            scalar1=float(c.S) + 0.5, scalar2=NEG_BIG,
                            op0=ALU.is_gt, op1=ALU.mult).then_inc(s_v.h, 1)
                    else:
                        v.tensor_scalar(
                            out=self.maskt[:, 512 * idx:512 * idx + 512],
                            in0=pa[:, 0:512],
                            scalar1=self.iabs[:, s:s + 1], scalar2=NEG_BIG,
                            op0=ALU.is_gt, op1=ALU.mult).then_inc(s_v.h, 1)
                    cnt += 1
            s_t.v += 1 + len(mspecs)
            s_v.v += 1 + c.NSLOT + len(mspecs)

    def p4_attn(self):
        c, nc = self.c, self.nc
        s_ld, s_st, s_t, s_v, s_a = self.s_ld, self.s_st, self.s_t, self.s_v, self.s_a
        scale = 1.0 / math.sqrt(128.0)
        ngrp = c.S // 1024
        SC = c.S
        with nc.Block() as blk:
            for h in range(c.H):
                ld0 = s_ld.v
                t_head_pre = s_t.v

                @blk.sync
                def _(sy, h=h, t_head_pre=t_head_pre):
                    if h > 0:
                        sy.wait_ge(s_t.h, t_head_pre)
                    for sg in range(ngrp):
                        sy.dma_start(
                            out=ap3(self.kh, 1024 * sg, (c.S, 128), (128, 8), (1, 128)),
                            in_=ap3(self.kv_out, 128 * h * c.TPC + 128 * sg,
                                    (c.TPC, 128), (c.KVN, 8), (1, 128))
                            ).then_inc(s_ld.h, 16)
                        sy.dma_start(
                            out=ap3(self.vh, 1024 * sg, (c.S, 128), (128, 8), (1, 128)),
                            in_=ap3(self.kv_out, c.VOFF + 128 * sg * c.D + 128 * h,
                                    (c.D, 128), (c.KVN, 8), (1, 128))
                            ).then_inc(s_ld.h, 16)
                    sy.dma_start(out=self.qh[:, :],
                                 in_=ap2(self.q_dr, 128 * h * c.TPC, (c.TPC, 128), (1, c.TPC))
                                 ).then_inc(s_ld.h, 16)
                s_ld.v += 16 * (2 * ngrp + 1)
                ld_end = s_ld.v

                tpv_end = []
                stt_end = []
                norm_end = []

                def stage_a(s, h=h, ld_end=ld_end, tpv_end=tpv_end, stt_end=stt_end):
                    nch = c.chunks_per_slot[s]
                    par = s % 2
                    t_pre, v_pre, a_pre = s_t.v, s_v.v, s_a.v
                    def mask_idx(kk, s=s):
                        if kk >= 2 * s:
                            return 2 + 2 * s + (kk - 2 * s)
                        if kk < 2:
                            return kk
                        return None

                    def evac_pos(kk, v_pre=v_pre, mask_idx=mask_idx):
                        p = v_pre
                        for k2 in range(kk + 1):
                            p += 1 if mask_idx(k2) is None else 2
                        return p
                    nmsk_all = sum(1 for kk in range(nch) if mask_idx(kk) is not None)
                    stt_end.append(v_pre + nch + nmsk_all)
                    prev_gate = stt_end[s - 1] if s >= 1 else 0

                    @blk.tensor
                    def _(t, s=s, nch=nch, v_pre=v_pre, ld_end=ld_end, prev_gate=prev_gate,
                          evac_pos=evac_pos):
                        t.wait_ge(s_ld.h, ld_end)
                        for kk in range(nch):
                            pa = self.psA if kk % 2 == 0 else self.psA2
                            if kk >= 2:
                                t.wait_ge(s_v.h, evac_pos(kk - 2))
                            elif prev_gate > 0:
                                t.wait_ge(s_v.h, prev_gate)
                            t.matmul(pa[:, 0:512], self.qh[:, 128 * s:128 * s + 128],
                                     self.kh[:, 512 * kk:512 * kk + 512],
                                     start=True, stop=True).then_inc(s_t.h, 1)

                    @blk.vector
                    def _(v, s=s, nch=nch, par=par, t_pre=t_pre, v_pre=v_pre, a_pre=a_pre):
                        v.wait_ge(s_a.h, a_pre)
                        if s >= 2:
                            v.wait_ge(s_t.h, tpv_end[s - 2])
                        cnt = v_pre
                        for kk in range(nch):
                            pa = self.psA if kk % 2 == 0 else self.psA2
                            v.wait_ge(s_t.h, t_pre + kk + 1)
                            mi = mask_idx(kk)
                            sl = self.scores[:, SC * par + 512 * kk:SC * par + 512 * kk + 512]
                            cm = self.cmax[:, c.NJC * par + kk:c.NJC * par + kk + 1]
                            if mi is None:
                                v.tensor_scalar(
                                    out=sl, in0=pa[:, 0:512], scalar1=scale, scalar2=None,
                                    op0=ALU.mult, op1=ALU.max,
                                    accum_out=cm).then_inc(s_v.h, 1)
                                cnt += 1
                            else:
                                v.scalar_tensor_tensor(
                                    out=sl, in0=pa[:, 0:512], scalar=scale,
                                    in1=self.maskt[:, 512 * mi:512 * mi + 512],
                                    op0=ALU.mult, op1=ALU.add).then_inc(s_v.h, 1)
                                cnt += 1
                                v.wait_ge(s_v.h, cnt)
                                v.reduce_max(cm, sl, axis=AX.X).then_inc(s_v.h, 1)
                                cnt += 1
                        v.wait_ge(s_v.h, cnt)
                        v.reduce_max(self.nmax[:, par:par + 1],
                                     self.cmax[:, c.NJC * par:c.NJC * par + nch],
                                     axis=AX.X, negate=True).then_inc(s_v.h, 1)
                    s_t.v += nch
                    nmsk = sum(1 for kk in range(nch) if mask_idx(kk) is not None)
                    s_v.v += nch + nmsk + 1

                    @blk.scalar
                    def _(sc, s=s, nch=nch, par=par):
                        sc.wait_ge(s_v.h, s_v.v)
                        sc.activation(self.scores[:, SC * par:SC * par + 512 * nch],
                                      self.scores[:, SC * par:SC * par + 512 * nch],
                                      AF.Exp, bias=self.nmax[:, par:par + 1], scale=1.0,
                                      accum_out=self.den[:, 4 * par:4 * par + 1]
                                      ).then_inc(s_a.h, 1)
                    s_a.v += 1

                    v_pre = s_v.v

                    @blk.vector
                    def _(v, s=s, par=par, v_pre=v_pre):
                        v.wait_ge(s_a.h, s_a.v)
                        cnt = v_pre
                        d0 = 4 * par

                        def step(op):
                            nonlocal cnt
                            op().then_inc(s_v.h, 1)
                            cnt += 1
                            v.wait_ge(s_v.h, cnt)
                        step(lambda: v.reciprocal(self.den[:, d0 + 1:d0 + 2],
                                                  self.den[:, d0:d0 + 1]))
                        step(lambda: v.tensor_tensor(out=self.den[:, d0 + 2:d0 + 3],
                                                     in0=self.den[:, d0:d0 + 1],
                                                     in1=self.den[:, d0 + 1:d0 + 2], op=ALU.mult))
                        step(lambda: v.tensor_scalar(out=self.den[:, d0 + 2:d0 + 3],
                                                     in0=self.den[:, d0 + 2:d0 + 3],
                                                     scalar1=-1.0, scalar2=2.0,
                                                     op0=ALU.mult, op1=ALU.add))
                        step(lambda: v.tensor_tensor(out=self.den[:, d0 + 3:d0 + 4],
                                                     in0=self.den[:, d0 + 1:d0 + 2],
                                                     in1=self.den[:, d0 + 2:d0 + 3], op=ALU.mult))
                    s_v.v = v_pre + 4

                    @blk.scalar
                    def _(sc, s=s, nch=nch, par=par):
                        sc.wait_ge(s_v.h, s_v.v)
                        sc.activation(self.scores[:, SC * par:SC * par + 512 * nch],
                                      self.scores[:, SC * par:SC * par + 512 * nch],
                                      AF.Identity, bias=0.0,
                                      scale=self.den[:, 4 * par + 3:4 * par + 4]
                                      ).then_inc(s_a.h, 1)
                    s_a.v += 1
                    norm_end.append(s_a.v)

                def stage_b(s, h=h, tpv_end=tpv_end):
                    nch = c.chunks_per_slot[s]
                    par = s % 2
                    t_pre, v_pre = s_t.v, s_v.v
                    a_pre = norm_end[s]

                    @blk.tensor
                    def _(t, s=s, nch=nch, par=par, t_pre=t_pre, v_pre=v_pre, a_pre=a_pre):
                        t.wait_ge(s_a.h, a_pre)
                        for kk in range(nch + 1):
                            if kk < nch:
                                pst = self.psT1 if kk % 2 == 0 else self.psT2
                                if kk >= 2:
                                    t.wait_ge(s_v.h, v_pre + kk - 1)
                                for q in range(4):
                                    jj = 4 * kk + q
                                    inst = t.transpose(
                                        pst[:, 128 * q:128 * q + 128],
                                        self.scores[:, SC * par + 128 * jj:SC * par + 128 * jj + 128],
                                        self.iden[:, :])
                                inst.then_inc(s_t.h, 1)
                            if kk >= 1:
                                t.wait_ge(s_v.h, v_pre + kk)
                                pb = 512 * ((kk - 1) % 2)
                                for q in range(4):
                                    jj = 4 * (kk - 1) + q
                                    inst = t.matmul(
                                        self.psAt[:, 0:128],
                                        self.vh[:, 128 * jj:128 * jj + 128],
                                        self.ptbuf[:, pb + 128 * q:pb + 128 * q + 128],
                                        start=(jj == 0), stop=(jj == 4 * nch - 1))
                                inst.then_inc(s_t.h, 1)

                    @blk.vector
                    def _(v, s=s, nch=nch, t_pre=t_pre, v_pre=v_pre):
                        for kk in range(nch):
                            pst = self.psT1 if kk % 2 == 0 else self.psT2
                            v.wait_ge(s_t.h, t_pre + 2 * kk + (1 if kk >= 1 else 1))
                            if kk >= 2:
                                v.wait_ge(s_t.h, t_pre + 2 * (kk - 1))
                            v.tensor_copy(self.ptbuf[:, 512 * (kk % 2):512 * (kk % 2) + 512],
                                          pst[:, 0:512]).then_inc(s_v.h, 1)
                    s_t.v += 2 * nch
                    s_v.v += nch
                    tpv_end.append(s_t.v)

                    @blk.vector
                    def _(v, s=s):
                        v.wait_ge(s_t.h, s_t.v)
                        if s == 0:
                            v.wait_ge(s_st.h, s_st.v)
                        v.tensor_copy(self.aout[:, 128 * s:128 * s + 128], self.psAt[:, 0:128]
                                      ).then_inc(s_v.h, 1)
                    s_v.v += 1

                stage_a(0)
                for s in range(1, c.NSLOT):
                    stage_a(s)
                    stage_b(s - 1)
                stage_b(c.NSLOT - 1)

                @blk.scalar
                def _(sc, h=h):
                    sc.wait_ge(s_v.h, s_v.v)
                    sc.dma_start(out=ap2(self.at_dr, 128 * h * c.TPC, (c.TPC, 128), (1, c.TPC)),
                                 in_=self.aout[:, :]).then_inc(s_st.h, 16)
                s_st.v += 16

    def p5_oproj(self):
        c, nc = self.c, self.nc
        s_ld, s_st, s_v = self.s_ld, self.s_st, self.s_v
        with nc.Block() as blk:
            st0 = s_st.v
            v_pre = s_v.v
            splits = []

            @blk.sync
            def _(sy, st0=st0, v_pre=v_pre):
                sy.wait_ge(s_st.h, st0)
                for cj in range(c.CT):
                    if cj >= 2:
                        sy.wait_ge(s_v.h, v_pre + 2 * (cj - 1))
                    sy.dma_start(out=self.xt_buf[:, c.TPC * (cj % 2):c.TPC * (cj % 2) + c.TPC],
                                 in_=ap2(self.at_dr, 128 * cj * c.TPC, (c.TPC, 128), (1, c.TPC))
                                 ).then_inc(self.s_x.h, 16)
                    self.s_x.v += 16
                    splits.append(self.s_x.v)
                for cj in range(c.CT):
                    splits[cj] = splits[min(cj | 1, c.CT - 1)]

            @blk.vector
            def _(v, v_pre=v_pre):
                cnt = v_pre
                for cj in range(c.CT):
                    v.wait_ge(self.s_x.h, splits[cj])
                    xs = self.xt_buf[:, c.TPC * (cj % 2):c.TPC * (cj % 2) + c.TPC]
                    zh = self.z_hi[:, cj * c.TPC:(cj + 1) * c.TPC]
                    zl = self.z_lo[:, cj * c.TPC:(cj + 1) * c.TPC]
                    v.tensor_copy(zh, xs).then_inc(s_v.h, 1)
                    cnt += 1
                    v.wait_ge(s_v.h, cnt)
                    v.scalar_tensor_tensor(out=zl, in0=zh, scalar=-1.0, in1=xs,
                                           op0=ALU.mult, op1=ALU.add).then_inc(s_v.h, 1)
                    cnt += 1
            s_v.v += 2 * c.CT
            self.proj_pass(blk, self.wo, self.x2_dr, self.x1_dr,
                           half_bias=False, lora=False)

    def p7_finish(self):
        with self.nc.Block() as blk:
            @blk.sync
            def _(sy):
                sy.wait_ge(self.s_st.h, self.s_st.v)
                sy.wait_ge(self.s_ld.h, self.s_ld.v)

            @blk.gpsimd
            def _(g):
                g.wait_ge(self.s_gd.h, self.s_gd.v)


def build_nc(cfg=None):
    return Prog(cfg or full_cfg()).build()


def make_in_maps(cfg, x_full, q_w, k_w, v_w, o_w, lora_A, lora_B, W, bias, lora_w):
    c = _C(cfg)
    x = np.ascontiguousarray(np.asarray(x_full, dtype=np.float32).reshape(c.S, c.D))
    xt = x.reshape(c.S // 128, 128, c.D)
    wq = np.ascontiguousarray(np.asarray(q_w, np.float32).T)
    wk = np.ascontiguousarray(np.asarray(k_w, np.float32).T)
    wv = np.ascontiguousarray(np.asarray(v_w, np.float32).T)
    wo = np.ascontiguousarray(np.asarray(o_w, np.float32).T)
    wf = np.ascontiguousarray(np.asarray(W, np.float32).T)
    acat = np.ascontiguousarray(np.asarray(lora_A, np.float32).reshape(128, c.D).T)
    bcat = np.ascontiguousarray(
        np.asarray(lora_B, np.float32).transpose(0, 2, 1).reshape(128, c.D))
    bias = np.ascontiguousarray(np.asarray(bias, np.float32))
    lw_rep = np.ascontiguousarray(np.repeat(np.asarray(lora_w, np.float32), c.RANK))
    in_maps = []
    for ci in range(8):
        shard = np.ascontiguousarray(xt[ci::8].reshape(c.TPC, c.D))
        tstart = (128.0 * (8 * np.arange(c.NSLOT) + ci)).astype(np.float32)
        tmask = np.zeros(c.TPC, np.float32)
        for s in range(c.NSLOT):
            t0 = 128 * (8 * s + ci)
            lo = max(c.INSTR_START, t0) - t0
            hi = min(c.instr_end, t0 + 128) - t0
            if hi > lo:
                tmask[128 * s + lo:128 * s + hi] = 1.0
        in_maps.append({
            "x": shard, "wqT": wq, "wkT": wk, "wvT": wv, "woT": wo, "WT": wf,
            "AcatT": acat, "BcatT": bcat, "bias": bias, "lw_rep": lw_rep,
            "tstart": tstart, "textmask": tmask,
        })
    return in_maps


def assemble_out(cfg, results):
    c = _C(cfg)
    out = np.empty((c.S // 128, 128, c.D), np.float32)
    for ci in range(8):
        out[ci::8] = np.asarray(results[ci]["y"]).reshape(c.NSLOT, 128, c.D)
    return out.reshape(1, c.S, c.D)


def kernel(L0_attn_hidden_states, L0_attn_q_w, L0_attn_k_w, L0_attn_v_w, L0_attn_o_w,
           L0_ffn_lora_A, L0_ffn_lora_B, L0_ffn_W, L0_ffn_bias, L0_ffn_lora_weights):
    cfg = full_cfg()
    in_maps = make_in_maps(cfg, L0_attn_hidden_states, L0_attn_q_w, L0_attn_k_w,
                           L0_attn_v_w, L0_attn_o_w, L0_ffn_lora_A, L0_ffn_lora_B,
                           L0_ffn_W, L0_ffn_bias, L0_ffn_lora_weights)
    nc = build_nc(cfg)
    res = run_bass_kernel_spmd(nc, in_maps, core_ids=list(range(8)))
    return assemble_out(cfg, res.results)
